# version 01
# baseline (speedup 1.0000x reference)
import numpy as np
import ml_dtypes
from contextlib import ExitStack
import concourse.bass as bass
import concourse.mybir as mybir
from concourse.bass_utils import run_bass_kernel_spmd

F32 = mybir.dt.float32
BF16 = mybir.dt.bfloat16
I32 = mybir.dt.int32
AF = mybir.ActivationFunctionType
OP = mybir.AluOpType
AX = mybir.AxisListType.X

NCORE = 8
SEQ = 4096
TOK = 8192
D = 1024
DCH = 8
E = 2048
ECH = 16
H = 4096
HCH = 32
TT = 512
NTILE = TOK // TT
TPB = SEQ // TT
NEXP = 8
CAP = 2048
QC = CAP // 128
CAPG = 8192
BIG = 65536.0
EPS = 1e-5

ENG_ORDER = ("sync", "tensor", "scalar", "vector", "gpsimd")


class Sem:
    def __init__(self, nc, stack, name):
        self.h = stack.enter_context(nc.semaphore(name))
        self.n = 0


class Prog:

    def __init__(self, nc):
        self.nc = nc
        self.q = {e: [] for e in ENG_ORDER}

    def op(self, eng, build, inc=None, k=1):
        if inc is not None:
            inc.n += k
            h = inc.h
            self.q[eng].append(lambda e: build(e).then_inc(h, k))
            return inc.n
        self.q[eng].append(lambda e: build(e))
        return None

    def fence(self, eng, fs=None):
        self.q[eng].append(lambda e: e.drain())

    def wait(self, eng, sem, val):
        if val is None or val <= 0:
            return
        h = sem.h
        self.q[eng].append(lambda e: e.wait_ge(h, val))

    def run(self, block):
        for name in ENG_ORDER:
            ops = self.q[name]
            if not ops:
                continue

            def body(e, ops=ops):
                for fn in ops:
                    fn(e)
            getattr(block, name)(body)
        self.q = {e: [] for e in ENG_ORDER}


def ts(i, n=128):
    return slice(i * n, (i + 1) * n)


def cs(i, w):
    return slice(i * w, (i + 1) * w)


class Ctx:
    pass


def emit_join(c, extra=()):
    p, j = c.p, c.join
    for sem, val in extra:
        p.wait("gpsimd", sem, val)
    for eng in ENG_ORDER:
        p.q[eng].append(lambda e: e.drain())
        p.op(eng, lambda e: e.nop(), inc=j)
    return j.n


def wait_join(c, val):
    for eng in ENG_ORDER:
        c.p.wait(eng, c.join, val)



def alloc_common(nc, stack, c):
    c.nc = nc
    c.stack = stack
    c.p = Prog(nc)
    c.join = Sem(nc, stack, "join")
    c.pm = [stack.enter_context(nc.psum_tensor(f"pm{i}", [128, 512], F32)) for i in range(4)]
    c.ptA = stack.enter_context(nc.psum_tensor("ptA", [128, 512], BF16))
    c.ptB = stack.enter_context(nc.psum_tensor("ptB", [128, 512], BF16))
    c.ptf = stack.enter_context(nc.psum_tensor("ptf", [128, 512], F32))
    c.pg = stack.enter_context(nc.psum_tensor("pg", [128, 512], F32))
    c.ident_bf = stack.enter_context(nc.sbuf_tensor("ident_bf", [128, 128], BF16))
    c.ident_f = stack.enter_context(nc.sbuf_tensor("ident_f", [128, 128], F32))
    c.tri_bf = stack.enter_context(nc.sbuf_tensor("tri_bf", [128, 128], BF16))
    c.ones1_bf = stack.enter_context(nc.sbuf_tensor("ones1_bf", [1, 128], BF16))
    c.onesc_bf = stack.enter_context(nc.sbuf_tensor("onesc_bf", [128, 1], BF16))
    c.ones1_f = stack.enter_context(nc.sbuf_tensor("ones1_f", [1, 128], F32))
    c.eids = stack.enter_context(nc.sbuf_tensor("eids", [128, 8], F32))
    c.slot_iota = stack.enter_context(nc.sbuf_tensor("slot_iota", [128, QC], F32))
    c.prefmask = stack.enter_context(nc.sbuf_tensor("prefmask", [8, 1], F32))
    c.bigfill = stack.enter_context(nc.sbuf_tensor("bigfill", [128, 2 * NEXP * CAP // 128], F32))
    c.thr_bc = stack.enter_context(nc.sbuf_tensor("thr_bc", [128, 8], F32))
    s = Sem(nc, stack, "s_init")
    p, d = c.p, c.dram
    for name, dst in [("ident_bf", c.ident_bf), ("ident_f", c.ident_f),
                      ("tri_bf", c.tri_bf), ("ones1_bf", c.ones1_bf), ("onesc_bf", c.onesc_bf),
                      ("ones1_f", c.ones1_f), ("eids", c.eids),
                      ("slot_iota", c.slot_iota), ("prefmask", c.prefmask)]:
        p.op("sync", lambda e, dst=dst, name=name: e.dma_start(out=dst[:, :], in_=d[name][:, :]), inc=s, k=16)
    p.op("gpsimd", lambda e: e.memset(c.bigfill[:, :], BIG))
    c.s_init = s
    c.init_target = s.n


def const_inputs():
    con = {}
    con["ident_bf"] = np.eye(128, dtype=ml_dtypes.bfloat16)
    con["ident_f"] = np.eye(128, dtype=np.float32)
    con["tri_bf"] = np.triu(np.ones((128, 128), np.float32), 1).astype(ml_dtypes.bfloat16)
    con["ones1_bf"] = np.ones((1, 128), ml_dtypes.bfloat16)
    con["onesc_bf"] = np.ones((128, 1), ml_dtypes.bfloat16)
    con["ones1_f"] = np.ones((1, 128), np.float32)
    con["eids"] = np.broadcast_to(np.arange(8, dtype=np.float32), (128, 8)).copy()
    con["slot_iota"] = (np.arange(128, dtype=np.float32)[:, None]
                        + 128.0 * np.arange(QC, dtype=np.float32)[None, :]).copy()
    con["eids4"] = np.tile(np.arange(8, dtype=np.float32), 4).reshape(1, 32)
    con["p_iota"] = np.arange(128, dtype=np.float32).reshape(128, 1)
    return con



def emit_l0_attn(c):
    nc, p, stack, d = c.nc, c.p, c.stack, c.dram
    ph = ExitStack()
    sb = lambda n, shp, dt: ph.enter_context(nc.sbuf_tensor(n, shp, dt))
    xi = [sb(f"a_xi{i}", [128, 4 * D], F32) for i in range(2)]
    xln = sb("a_xln", [128, 4 * D], BF16)
    xlnT = sb("a_xlnT", [128, DCH * TT], BF16)
    xc = sb("a_xc", [128, ECH * TT], BF16)
    sig = sb("a_sig", [128, ECH * TT], F32)
    u = sb("a_u", [128, ECH * TT], F32)
    xpt = [sb(f"a_xpt{i}", [128, TT + 3], F32) for i in range(2)]
    xcf = [sb(f"a_xcf{i}", [128, TT], F32) for i in range(2)]
    sigd = sb("a_sigd", [128, 4 * TT], F32)
    oTc = [sb(f"a_oTc{i}", [128, TT], F32) for i in range(2)]
    halo = sb("a_halo", [128, 3 * ECH], F32)
    carry = sb("a_carry", [128, ECH], F32)
    musum = [sb(f"a_mus{i}", [128, 4], F32) for i in range(2)]
    m2sum = [sb(f"a_m2s{i}", [128, 4], F32) for i in range(2)]
    lnt = [sb(f"a_lnt{i}", [128, 36], F32) for i in range(2)]
    wb = [sb(f"a_wb{i}", [128, 16 * 512], BF16) for i in range(2)]
    g_rep = sb("a_grep", [128, D], F32)
    b_rep = sb("a_brep", [128, D], F32)
    b_in_t = sb("a_bin", [128, 32], F32)
    convw = sb("a_convw", [128, 4 * ECH], F32)
    convb = sb("a_convb", [128, ECH], F32)
    b_c_t = sb("a_bct", [128, ECH], F32)
    b_d_t = sb("a_bdt", [128, ECH], F32)
    b_o_t = sb("a_bot", [128, 8], F32)
    ldl = sb("a_ldl", [128, ECH], F32)
    a_dec = sb("a_adec", [128, ECH], F32)
    abm = sb("a_abm", [128, 16 * ECH], F32)
    bm_t = sb("a_bmt", [128, 16 * ECH], F32)
    ccoef = sb("a_ccoef", [128, ECH], F32)

    sx = [Sem(nc, stack, "a_sx0"), Sem(nc, stack, "a_sx1")]
    sw = [Sem(nc, stack, "a_sw0"), Sem(nc, stack, "a_sw1")]
    sp = Sem(nc, stack, "a_sp")
    mm = Sem(nc, stack, "a_mm")
    ac = Sem(nc, stack, "a_ac")
    dv = Sem(nc, stack, "a_dv")
    so = [Sem(nc, stack, "a_so0"), Sem(nc, stack, "a_so1")]
    fdv = Sem(nc, stack, "a_fdv")
    fac = Sem(nc, stack, "a_fac")

    for name, dst in [("l0_b_in_t", b_in_t), ("l0_convw", convw), ("l0_convb", convb),
                      ("l0_b_c_t", b_c_t), ("l0_b_d_t", b_d_t), ("l0_b_out_t", b_o_t),
                      ("l0_logdelta_t", ldl), ("l0_a_t", abm), ("l0_bm_t", bm_t)]:
        p.op("sync", lambda e, n=name, t=dst: e.dma_start(out=t[:, :], in_=d[n][:, :]), inc=sp, k=16)
    p.op("sync", lambda e: e.dma_start(out=g_rep[:, :], in_=d["l0_ln_g"][0:1, :].broadcast_to((128, D))), inc=sp, k=16)
    p.op("sync", lambda e: e.dma_start(out=b_rep[:, :], in_=d["l0_ln_b"][0:1, :].broadcast_to((128, D))), inc=sp, k=16)
    p_ld = sp.n
    pg2_free = None
    p.wait("scalar", sp, p_ld)
    ac_adec = p.op("scalar", lambda e: e.activation(out=a_dec[:, :], in_=ldl[:, :],
                                                    func=AF.Sigmoid, scale=-1.0), inc=ac)
    p.wait("vector", sp, p_ld)
    p.op("vector", lambda e: e.tensor_tensor(out=abm[:, :], in0=abm[:, :], in1=bm_t[:, :], op=OP.mult))
    p.op("vector", lambda e: e.tensor_reduce(
        out=ccoef[:, :], in_=abm[:, :].rearrange("p (c n) -> p c n", n=16), axis=AX, op=OP.add))
    p.op("vector", lambda e: e.memset(halo[:, :], 0.0))
    dv_cc = p.op("vector", lambda e: e.memset(carry[:, :], 0.0), inc=dv)

    stages = ([("l0_w_in_t", og, 8, "win") for og in range(8)]
              + sum([[("l0_w_d_t", og, 16, "wd"), ("l0_w_c_t", og, 16, "wc")] for og in range(4)], [])
              + [("l0_w_out_t", og, 16, "wout") for og in range(2)])
    NS = len(stages)

    stage_done_mm = {}
    stage_sw = {}
    bank_free = [None] * 4
    tb_free = {"ptA": None, "ptB": None, "ptf": None, "pg": None}
    oTc_free = [None, None]
    xr_done = {}
    store_done = {}
    prev_dv_xg = 0
    pending_store = None
    mm_T = {}

    def load_stage(i, s_i):
        gidx = i * NS + s_i
        wname, og, nk, kind = stages[s_i]
        if gidx >= 2 and (gidx - 2) in stage_done_mm:
            p.wait("gpsimd", mm, stage_done_mm[gidx - 2])
        buf = wb[gidx % 2]
        swp = sw[gidx % 2]
        p.op("gpsimd", lambda e, wname=wname, og=og, nk=nk, buf=buf: e.dma_start(
            out=buf[:, 0:nk * 512].rearrange("p (k q) -> p k q", q=512),
            in_=d[wname][0:nk * 128, cs(og, 512)].rearrange("(k p) q -> p k q", p=128)), inc=swp, k=16)
        stage_sw[gidx] = (swp, swp.n)

    for i in range(NTILE):
        par = i % 2
        xib = xi[par]
        mus, m2s, lt = musum[par], m2sum[par], lnt[par]
        fob = (i % TPB == 0)
        col = lambda j: lt[:, 4 * j:4 * j + 4]
        MU, V_, VE, DEN, R1, R2, R12, R1P1, MUR1 = [col(j) for j in range(9)]

        if i >= 2:
            p.wait("sync", *store_done[i - 2])
        sxp = sx[par]
        p.op("sync", lambda e, i=i, xib=xib: e.dma_start(
            out=xib[:, :].rearrange("p (s q) -> p s q", s=4),
            in_=d["x_in"][i * TT:(i + 1) * TT, :].rearrange("(s p) q -> p s q", p=128)), inc=sxp, k=16)
        x_ld = (sxp, sxp.n)
        load_stage(i, 0)
        load_stage(i, 1)

        p.wait("vector", *x_ld)
        p.wait("scalar", *x_ld)
        if i == 0:
            p.wait("vector", c.s_init, c.init_target)
            p.wait("scalar", c.s_init, c.init_target)
        if i >= 1:
            p.wait("scalar", dv, prev_dv_xg)
        if i >= 1:
            p.wait("vector", mm, mm_T[i - 1])
        for s in range(4):
            p.op("vector", lambda e, s=s, xib=xib, mus=mus: e.tensor_reduce(
                out=mus[:, s:s + 1], in_=xib[:, cs(s, D)], axis=AX, op=OP.add))
            p.op("scalar", lambda e, s=s, xib=xib, m2s=m2s: e.activation(
                out=u[:, 0:D], in_=xib[:, cs(s, D)], func=AF.Square,
                accum_out=m2s[:, s:s + 1]), inc=ac)
        ac_sq = ac.n
        if pending_store is not None:
            pj, pxib = pending_store
            sop = so[pj % 2]
            p.wait("scalar", dv, xr_done[pj])
            p.op("scalar", lambda e, pj=pj, pxib=pxib: e.dma_start(
                out=d["xB"][pj * TT:(pj + 1) * TT, :].rearrange("(s p) q -> p s q", p=128),
                in_=pxib[:, :].rearrange("p (s q) -> p s q", s=4)), inc=sop, k=16)
            store_done[pj] = (sop, sop.n)
            pending_store = None
        p.fence("vector", fdv)
        p.op("vector", lambda e, mus=mus, MU=MU: e.tensor_scalar(
            out=MU, in0=mus[:, :], scalar1=1.0 / D, scalar2=0.0, op0=OP.mult, op1=OP.add))
        p.fence("vector", fdv)
        p.op("vector", lambda e, MU=MU, V_=V_: e.tensor_tensor(out=V_, in0=MU, in1=MU, op=OP.mult))
        p.fence("vector", fdv)
        p.wait("vector", ac, ac_sq)
        p.op("vector", lambda e, m2s=m2s, V_=V_: e.scalar_tensor_tensor(
            out=V_, in0=m2s[:, :], scalar=1.0 / D, in1=V_, op0=OP.mult, op1=OP.subtract))
        p.fence("vector", fdv)
        p.op("vector", lambda e, V_=V_, VE=VE: e.tensor_scalar(
            out=VE, in0=V_, scalar1=EPS, scalar2=0.0, op0=OP.add, op1=OP.add))
        p.fence("vector", fdv)
        p.op("vector", lambda e, VE=VE, V_=V_, DEN=DEN: e.scalar_tensor_tensor(
            out=DEN, in0=VE, scalar=EPS, in1=V_, op0=OP.mult, op1=OP.add))
        p.fence("vector", fdv)
        p.op("vector", lambda e, VE=VE, R1=R1: e.reciprocal(out=R1, in_=VE))
        p.op("vector", lambda e, DEN=DEN: e.reciprocal(out=DEN, in_=DEN))
        p.fence("vector", fdv)
        p.op("vector", lambda e, VE=VE, DEN=DEN, R2=R2: e.tensor_tensor(
            out=R2, in0=VE, in1=DEN, op=OP.mult))
        p.fence("vector", fdv)
        dv_l = p.op("vector", lambda e: e.nop(), inc=dv)
        p.wait("scalar", dv, dv_l)
        p.op("scalar", lambda e, R1=R1: e.activation(out=R1, in_=R1, func=AF.Sqrt))
        p.op("scalar", lambda e, R2=R2: e.activation(out=R2, in_=R2, func=AF.Sqrt))
        p.fence("scalar", fac)
        ac_r = p.op("scalar", lambda e: e.nop(), inc=ac)
        p.wait("vector", ac, ac_r)
        p.op("vector", lambda e, R1=R1, R2=R2, R12=R12: e.tensor_tensor(out=R12, in0=R1, in1=R2, op=OP.mult))
        p.op("vector", lambda e, R1=R1, R1P1=R1P1: e.tensor_scalar(
            out=R1P1, in0=R1, scalar1=1.0, scalar2=0.0, op0=OP.add, op1=OP.add))
        p.op("vector", lambda e, MU=MU, R1=R1, MUR1=MUR1: e.tensor_tensor(out=MUR1, in0=MU, in1=R1, op=OP.mult))
        p.fence("vector", fdv)
        for s in range(4):
            p.op("vector", lambda e, s=s, xib=xib, MU=MU, R12=R12: e.tensor_scalar(
                out=sig[:, 0:D], in0=xib[:, cs(s, D)], scalar1=MU[:, s:s + 1], scalar2=R12[:, s:s + 1],
                op0=OP.subtract, op1=OP.mult))
            p.op("vector", lambda e: e.tensor_tensor(out=sig[:, 0:D], in0=sig[:, 0:D], in1=g_rep[:, :], op=OP.mult))
            p.op("vector", lambda e, s=s: e.tensor_tensor(
                out=xln[:, cs(s, D)], in0=sig[:, 0:D], in1=b_rep[:, :], op=OP.add))
            p.op("vector", lambda e, s=s, xib=xib, R1P1=R1P1, MUR1=MUR1: e.tensor_scalar(
                out=xib[:, cs(s, D)], in0=xib[:, cs(s, D)], scalar1=R1P1[:, s:s + 1], scalar2=MUR1[:, s:s + 1],
                op0=OP.mult, op1=OP.subtract))
        p.fence("vector")
        dv_xln = p.op("vector", lambda e: e.nop(), inc=dv)

        p.wait("tensor", dv, dv_xln)
        for dc in range(DCH):
            key = "ptA" if dc % 2 == 0 else "ptB"
            bank = c.ptA if dc % 2 == 0 else c.ptB
            if tb_free[key] is not None:
                p.wait("tensor", *tb_free[key])
            for s in range(4):
                p.op("tensor", lambda e, dc=dc, s=s, bank=bank: e.transpose(
                    bank[:, cs(s, 128)], xln[:, cs(s * DCH + dc, 128)], c.ident_bf[:, :]),
                    inc=(mm if s == 3 else None))
            mv = mm.n
            p.wait("vector", mm, mv)
            ev = p.op("vector", lambda e, dc=dc, bank=bank: e.tensor_copy(
                xlnT[:, cs(dc, TT)], bank[:, :]), inc=dv)
            tb_free[key] = (dv, ev)
        mm_T[i] = mm.n
        p.fence("vector")
        dv_xlnT = p.op("vector", lambda e: e.nop(), inc=dv)

        sig_ac = {}
        sigd_ac = {}
        u_dv = {}
        conv_dv = {}
        silu_ac = {}
        for s_i in range(NS):
            wname, og, nk, kind = stages[s_i]
            gidx = i * NS + s_i
            buf = wb[gidx % 2]
            p.wait("tensor", *stage_sw[gidx])
            if kind == "win":
                p.wait("tensor", dv, dv_xlnT)
                rhs = xlnT
            elif kind in ("wd", "wc"):
                if s_i == 8:
                    p.fence("scalar")
                    ac_xc = p.op("scalar", lambda e: e.nop(), inc=ac)
                    p.wait("tensor", ac, ac_xc)
                rhs = xc
            else:
                rhs = xc
            for j in range(4):
                och = og * 4 + j
                bk = c.pm[j]
                if bank_free[j] is not None:
                    p.wait("tensor", *bank_free[j])
                for k in range(nk):
                    p.op("tensor", lambda e, k=k, j=j, bk=bk, buf=buf, rhs=rhs, nk=nk: e.matmul(
                        bk[:, :], buf[:, cs(k, 512)][:, cs(j, 128)], rhs[:, cs(k, TT)],
                        start=(k == 0), stop=(k == nk - 1)),
                        inc=(mm if k == nk - 1 else None))
                mv = mm.n
                if kind == "win" and och < ECH:
                    pp = och % 2
                    if och >= 2:
                        p.wait("scalar", dv, conv_dv[och - 2])
                    p.wait("scalar", mm, mv)
                    aev = p.op("scalar", lambda e, och=och, pp=pp, bk=bk: e.activation(
                        out=xpt[pp][:, 3:3 + TT], in_=bk[:, :], func=AF.Identity,
                        bias=b_in_t[:, och:och + 1]), inc=ac)
                    bank_free[j] = (ac, aev)
                    p.wait("vector", ac, aev)
                    if och >= 2:
                        p.wait("vector", ac, silu_ac[och - 2])
                    if fob:
                        p.op("vector", lambda e, pp=pp: e.memset(xpt[pp][:, 0:3], 0.0))
                    else:
                        p.op("vector", lambda e, pp=pp, och=och: e.tensor_copy(
                            xpt[pp][:, 0:3], halo[:, och * 3:och * 3 + 3]))
                    p.op("vector", lambda e, och=och, pp=pp: e.tensor_scalar(
                        out=xcf[pp][:, :], in0=xpt[pp][:, 3:3 + TT],
                        scalar1=convw[:, och * 4 + 3:och * 4 + 4], scalar2=convb[:, och:och + 1],
                        op0=OP.mult, op1=OP.add))
                    for kk in range(3):
                        p.op("vector", lambda e, och=och, pp=pp, kk=kk: e.scalar_tensor_tensor(
                            out=xcf[pp][:, :], in0=xpt[pp][:, kk:kk + TT],
                            scalar=convw[:, och * 4 + kk:och * 4 + kk + 1], in1=xcf[pp][:, :],
                            op0=OP.mult, op1=OP.add))
                    cdv = p.op("vector", lambda e, pp=pp, och=och: e.tensor_copy(
                        halo[:, och * 3:och * 3 + 3], xpt[pp][:, TT:TT + 3]), inc=dv)
                    conv_dv[och] = cdv
                    p.wait("scalar", dv, cdv)
                    sl = p.op("scalar", lambda e, och=och, pp=pp: e.activation(
                        out=xc[:, cs(och, TT)], in_=xcf[pp][:, :], func=AF.Silu), inc=ac)
                    silu_ac[och] = sl
                elif kind == "win":
                    gch = och - ECH
                    p.wait("scalar", mm, mv)
                    aev = p.op("scalar", lambda e, gch=gch, och=och, bk=bk: e.activation(
                        out=sig[:, cs(gch, TT)], in_=bk[:, :], func=AF.Sigmoid,
                        bias=b_in_t[:, och:och + 1]), inc=ac)
                    bank_free[j] = (ac, aev)
                    sig_ac[gch] = aev
                elif kind == "wd":
                    p.wait("scalar", mm, mv)
                    if och >= 4:
                        p.wait("scalar", dv, u_dv[och - 4])
                    aev = p.op("scalar", lambda e, j=j, och=och, bk=bk: e.activation(
                        out=sigd[:, cs(j, TT)], in_=bk[:, :], func=AF.Sigmoid,
                        bias=b_d_t[:, och:och + 1]), inc=ac)
                    bank_free[j] = (ac, aev)
                    sigd_ac[och] = aev
                elif kind == "wc":
                    p.wait("vector", mm, mv)
                    p.wait("vector", ac, sigd_ac[och])
                    dvv = p.op("vector", lambda e, j=j, och=och, bk=bk: e.scalar_tensor_tensor(
                        out=u[:, cs(och, TT)], in0=bk[:, :], scalar=b_c_t[:, och:och + 1],
                        in1=sigd[:, cs(j, TT)], op0=OP.add, op1=OP.mult), inc=dv)
                    bank_free[j] = (dv, dvv)
                    u_dv[och] = dvv
                else:
                    pp = j % 2
                    dc = och
                    p.wait("scalar", mm, mv)
                    if oTc_free[pp] is not None:
                        p.wait("scalar", mm, oTc_free[pp])
                    aev = p.op("scalar", lambda e, och=och, pp=pp, bk=bk: e.activation(
                        out=oTc[pp][:, :], in_=bk[:, :], func=AF.Identity,
                        bias=b_o_t[:, och:och + 1]), inc=ac)
                    bank_free[j] = (ac, aev)
                    key = "ptf" if dc % 2 == 0 else "pg"
                    tb = c.ptf if dc % 2 == 0 else c.pg
                    p.wait("tensor", ac, aev)
                    if tb_free[key] is not None:
                        p.wait("tensor", *tb_free[key])
                    for q in range(4):
                        p.op("tensor", lambda e, q=q, pp=pp, tb=tb: e.transpose(
                            tb[:, cs(q, 128)], oTc[pp][:, cs(q, 128)], c.ident_f[:, :]),
                            inc=(mm if q == 3 else None))
                    mtv = mm.n
                    oTc_free[pp] = mtv
                    p.wait("vector", mm, mtv)
                    dvv = p.op("vector", lambda e, dc=dc, tb=tb, xib=xib: e.tensor_tensor(
                        out=xib[:, :].rearrange("p (s t q) -> p s t q", s=4, t=DCH)[:, :, dc, :],
                        in0=tb[:, :].rearrange("p (s q) -> p s q", s=4),
                        in1=xib[:, :].rearrange("p (s t q) -> p s t q", s=4, t=DCH)[:, :, dc, :],
                        op=OP.add), inc=dv)
                    tb_free[key] = (dv, dvv)
            stage_done_mm[gidx] = mm.n
            if s_i + 2 < NS:
                load_stage(i, s_i + 2)

            if s_i == 15:
                if i == 0:
                    p.wait("vector", dv, dv_cc)
                    p.wait("vector", ac, ac_adec)
                if fob:
                    p.op("vector", lambda e: e.memset(carry[:, :], 0.0))
                    p.fence("vector", fdv)
                wc_done = mm.n
                p.wait("vector", mm, wc_done)
                for chn in range(ECH):
                    p.wait("vector", dv, u_dv[chn])
                    p.wait("vector", ac, sig_ac[chn])
                    p.op("vector", lambda e, chn=chn: e.tensor_tensor_scan(
                        out=u[:, cs(chn, TT)],
                        data0=a_dec[:, chn:chn + 1].broadcast_to((128, TT)),
                        data1=u[:, cs(chn, TT)],
                        initial=carry[:, chn:chn + 1],
                        op0=OP.mult, op1=OP.add))
                    p.op("vector", lambda e, chn=chn: e.scalar_tensor_tensor(
                        out=xc[:, cs(chn, TT)], in0=u[:, cs(chn, TT)], scalar=ccoef[:, chn:chn + 1],
                        in1=sig[:, cs(chn, TT)], op0=OP.mult, op1=OP.mult))
                    p.op("vector", lambda e, chn=chn: e.tensor_copy(
                        carry[:, chn:chn + 1], u[:, chn * TT + TT - 1:chn * TT + TT]))
                p.fence("vector")
                dv_xg = p.op("vector", lambda e: e.nop(), inc=dv)
                prev_dv_xg = dv_xg
                p.wait("tensor", dv, dv_xg)

        p.fence("vector")
        xr_done[i] = p.op("vector", lambda e: e.nop(), inc=dv)
        pending_store = (i, xib)
        if i == 0 and getattr(c, "dbg", None) is not None:
            dbg = c.dbg
            p.wait("gpsimd", dv, dv.n)
            p.wait("gpsimd", ac, ac.n)
            p.wait("gpsimd", mm, mm.n)
            for nm, buf in [("dbg_xln", xln), ("dbg_xg", xc), ("dbg_sig", sig),
                            ("dbg_s", u), ("dbg_xlnT", xlnT), ("dbg_xr", xib),
                            ("dbg_lnt", lnt[0]), ("dbg_mus", musum[0]), ("dbg_m2s", m2sum[0])]:
                if nm in dbg:
                    p.op("gpsimd", lambda e, nm=nm, buf=buf: e.dma_start(
                        out=dbg[nm][:, :], in_=buf[:, :]), inc=so, k=16)

    pj, pxib = pending_store
    sop = so[pj % 2]
    p.wait("scalar", dv, xr_done[pj])
    p.op("scalar", lambda e, pj=pj, pxib=pxib: e.dma_start(
        out=d["xB"][pj * TT:(pj + 1) * TT, :].rearrange("(s p) q -> p s q", p=128),
        in_=pxib[:, :].rearrange("p (s q) -> p s q", s=4)), inc=sop, k=16)
    store_done[pj] = (sop, sop.n)
    emit_join(c, extra=[(so[0], so[0].n), (so[1], so[1].n), (sx[0], sx[0].n), (sx[1], sx[1].n),
                        (sw[0], sw[0].n), (sw[1], sw[1].n), (sp, sp.n)])
    ph.close()

GSZ = 4
NGRP = TOK // (GSZ * 128)
RND = 12582912.0


def emit_gate(c, layer):
    nc, p, stack, d = c.nc, c.p, c.stack, c.dram
    ph = ExitStack()
    L = layer
    xsrc = d["xB"] if L == 0 else d["xD"]
    xlnd = d["xlnB"] if L == 0 else d["xlnD"]
    tabd = d["tabB"] if L == 0 else d["tabD"]
    wg_name = "l0_wg" if L == 0 else "l1_wr"
    pfx = f"g{L}_"
    sb = lambda n, shp, dt: ph.enter_context(nc.sbuf_tensor(pfx + n, shp, dt))

    xg = [sb(f"xg{i}", [128, GSZ * D], F32) for i in range(2)]
    xlng = sb("xlng", [128, GSZ * D], BF16)
    xlnT = sb("xlnT", [128, DCH * 512], BF16)
    wg_sb = sb("wg", [128, DCH * 8], BF16)
    bg4 = sb("bg4", [128, GSZ * 8], F32) if L == 0 else None
    eids4 = sb("eids4", [128, GSZ * 8], F32)
    p_iota = sb("piota", [128, 1], F32)
    mus = [sb(f"mus{i}", [128, GSZ], F32) for i in range(2)]
    m2s = [sb(f"m2s{i}", [128, GSZ], F32) for i in range(2)]
    lnt = [sb(f"lnt{i}", [128, 4 * GSZ], F32) for i in range(2)]
    sqs = sb("sqs", [128, D], F32)
    lg = sb("lg", [128, GSZ * 8], F32)
    ex = sb("ex", [128, GSZ * 8], F32)
    smx = sb("smx", [128, 6 * GSZ], F32)
    mask = sb("mask", [128, GSZ * 8], F32)
    mask_bf = sb("maskbf", [128, GSZ * 8], BF16)
    offrow = sb("offrow", [1, GSZ * 8], F32)
    offbc = sb("offbc", [128, GSZ * 8], F32)
    rank_sb = sb("rank", [128, GSZ * 8], F32)
    tchain = sb("tchain", [128, 6 * GSZ], F32)
    desti = sb("desti", [128, GSZ], I32)
    rec = sb("rec", [128, GSZ * 2], F32)
    cnt_row = sb("cntrow", [1, 8], F32)
    runrow = sb("runrow", [1, 8], F32)
    cntb = sb("cntb", [1, 8], F32)
    cc_sb = sb("ccsb", [8, 8], F32) if L == 1 else None
    thr_row = sb("thrrow", [1, 8], F32) if L == 1 else None
    thr_bc = c.thr_bc if L == 1 else None

    sx = [Sem(nc, stack, pfx + "sx0"), Sem(nc, stack, pfx + "sx1")]
    sox = [Sem(nc, stack, pfx + "sox0"), Sem(nc, stack, pfx + "sox1")]
    sp = Sem(nc, stack, pfx + "sp")
    mm = Sem(nc, stack, pfx + "mm")
    ac = Sem(nc, stack, pfx + "ac")
    dv = Sem(nc, stack, pfx + "dv")
    so = Sem(nc, stack, pfx + "so")
    scc = Sem(nc, stack, pfx + "scc")
    spw = Sem(nc, stack, pfx + "spw")

    jv = c.join.n
    for eng in ENG_ORDER:
        p.wait(eng, c.join, jv)
        p.wait(eng, c.s_init, c.init_target)

    p.op("gpsimd", lambda e: e.dma_start(out=wg_sb[:, :], in_=d[wg_name][:, :].rearrange(
        "(k p) q -> p k q", p=128)), inc=spw, k=16)
    if L == 0:
        p.op("sync", lambda e: e.dma_start(out=bg4[:, :], in_=d["l0_bg4"][0:1, :].broadcast_to((128, GSZ * 8))), inc=sp, k=16)
    p.op("sync", lambda e: e.dma_start(out=eids4[:, :], in_=d["eids4"][0:1, :].broadcast_to((128, GSZ * 8))), inc=sp, k=16)
    p.op("sync", lambda e: e.dma_start(out=p_iota[:, :], in_=d["p_iota"][:, :]), inc=sp, k=16)
    p.op("sync", lambda e: e.dma_start(
        out=tabd[:, :].rearrange("a b -> (a b)").rearrange("(p g) -> p g", p=128),
        in_=c.bigfill[:, 0:2 * NEXP * CAP // 128]), inc=sp, k=16)
    p_ld = sp.n
    p.wait("tensor", spw, spw.n)
    bndh = {}
    def _init_bnd(e):
        r = e.alloc_register(pfx + "bnd")
        e.reg_mov(r, NEXP * CAP - 1)
        bndh["r"] = r
    p.q["gpsimd"].append(_init_bnd)
    p.wait("tensor", sp, p_ld)
    p.wait("vector", sp, p_ld)
    p.wait("scalar", sp, p_ld)
    p.wait("gpsimd", sp, p_ld)

    p.op("vector", lambda e: e.memset(runrow[0:1, :], 0.0))
    p.fence("vector")
    dv_run0 = 0
    tb_free = {"ptA": None, "ptB": None}
    off_mm = 0
    off_ac = 0
    xln_st = {}
    prev_grp_dv = 0

    for g in range(NGRP):
        par = g % 2
        xgb = xg[par]
        MUS, M2S, LT = mus[par], m2s[par], lnt[par]
        MU = LT[:, 0:GSZ]
        VE = LT[:, GSZ:2 * GSZ]
        R1 = LT[:, 2 * GSZ:3 * GSZ]

        if g >= 2:
            p.wait("sync", dv, c._g_done[g - 2])
        sxp = sx[par]
        p.op("sync", lambda e, g=g, xgb=xgb: e.dma_start(
            out=xgb[:, :].rearrange("p (s q) -> p s q", s=GSZ),
            in_=xsrc[g * 512:(g + 1) * 512, :].rearrange("(s p) q -> p s q", p=128)), inc=sxp, k=16)
        x_ld = (sxp, sxp.n)

        p.wait("vector", *x_ld)
        p.wait("scalar", *x_ld)
        if g >= 1:
            p.wait("vector", mm, c._g_mmT[g - 1])
            p.wait("vector", *xln_st[g - 1])
        for j in range(GSZ):
            p.op("vector", lambda e, j=j, xgb=xgb, MUS=MUS: e.tensor_reduce(
                out=MUS[:, j:j + 1], in_=xgb[:, cs(j, D)], axis=AX, op=OP.add))
            p.op("scalar", lambda e, j=j, xgb=xgb, M2S=M2S: e.activation(
                out=sqs[:, :], in_=xgb[:, cs(j, D)], func=AF.Square,
                accum_out=M2S[:, j:j + 1]), inc=ac)
        ac_sq = ac.n
        p.fence("vector")
        p.op("vector", lambda e, MUS=MUS, MU=MU: e.tensor_scalar(
            out=MU, in0=MUS[:, :], scalar1=1.0 / D, scalar2=0.0, op0=OP.mult, op1=OP.add))
        p.fence("vector")
        p.op("vector", lambda e, MU=MU, VE=VE: e.tensor_tensor(out=VE, in0=MU, in1=MU, op=OP.mult))
        p.fence("vector")
        p.wait("vector", ac, ac_sq)
        p.op("vector", lambda e, M2S=M2S, VE=VE: e.scalar_tensor_tensor(
            out=VE, in0=M2S[:, :], scalar=1.0 / D, in1=VE, op0=OP.mult, op1=OP.subtract))
        p.fence("vector")
        p.op("vector", lambda e, VE=VE: e.tensor_scalar(
            out=VE, in0=VE, scalar1=EPS, scalar2=0.0, op0=OP.add, op1=OP.add))
        p.fence("vector")
        dvv = p.op("vector", lambda e: e.nop(), inc=dv)
        p.wait("scalar", dv, dvv)
        p.op("scalar", lambda e, VE=VE, R1=R1: e.activation(out=R1, in_=VE, func=AF.Ln))
        p.fence("scalar")
        p.op("scalar", lambda e, R1=R1: e.activation(out=R1, in_=R1, func=AF.Exp, scale=-0.5))
        p.fence("scalar")
        ac_r = p.op("scalar", lambda e: e.nop(), inc=ac)
        p.wait("vector", ac, ac_r)
        for j in range(GSZ):
            p.op("vector", lambda e, j=j, xgb=xgb, MU=MU, R1=R1: e.tensor_scalar(
                out=xlng[:, cs(j, D)], in0=xgb[:, cs(j, D)], scalar1=MU[:, j:j + 1],
                scalar2=R1[:, j:j + 1], op0=OP.subtract, op1=OP.mult))
        p.fence("vector")
        dv_xln = p.op("vector", lambda e: e.nop(), inc=dv)
        p.wait("gpsimd", dv, dv_xln)
        soxp = sox[par]
        p.op("gpsimd", lambda e, g=g: e.dma_start(
            out=xlnd[g * 512:(g + 1) * 512, :].rearrange("(s p) q -> p s q", p=128),
            in_=xlng[:, :].rearrange("p (s q) -> p s q", s=GSZ)), inc=soxp, k=16)
        xln_st[g] = (soxp, soxp.n)

        p.wait("tensor", dv, dv_xln)
        for dc in range(DCH):
            key = "ptA" if dc % 2 == 0 else "ptB"
            bank = c.ptA if dc % 2 == 0 else c.ptB
            if tb_free[key] is not None:
                p.wait("tensor", *tb_free[key])
            for j in range(GSZ):
                p.op("tensor", lambda e, dc=dc, j=j, bank=bank: e.transpose(
                    bank[:, cs(j, 128)], xlng[:, cs(j * DCH + dc, 128)], c.ident_bf[:, :]),
                    inc=(mm if j == GSZ - 1 else None))
            mv = mm.n
            p.wait("vector", mm, mv)
            ev = p.op("vector", lambda e, dc=dc, bank=bank: e.tensor_copy(
                xlnT[:, cs(dc, 512)], bank[:, :]), inc=dv)
            tb_free[key] = (dv, ev)
        c._g_mmT = getattr(c, "_g_mmT", {})
        c._g_mmT[g] = mm.n
        p.fence("vector")
        dv_xT = p.op("vector", lambda e: e.nop(), inc=dv)

        p.wait("tensor", dv, dv_xT)
        if g >= 1:
            p.wait("tensor", dv, c._g_lg_dv[g - 1])
        for j in range(GSZ):
            for dc in range(DCH):
                p.op("tensor", lambda e, j=j, dc=dc: e.matmul(
                    c.pg[:, j * 8:(j + 1) * 8], xlnT[:, cs(dc * GSZ + j, 128)],
                    wg_sb[:, cs(dc, 8)], start=(dc == 0), stop=(dc == DCH - 1)),
                    inc=(mm if (j == GSZ - 1 and dc == DCH - 1) else None))
        mm_lg = mm.n
        p.wait("vector", mm, mm_lg)
        if L == 0:
            p.op("vector", lambda e: e.tensor_tensor(out=lg[:, :], in0=c.pg[:, 0:GSZ * 8], in1=bg4[:, :], op=OP.add))
        else:
            p.op("vector", lambda e: e.tensor_copy(lg[:, :], c.pg[:, 0:GSZ * 8]))
        dv_lg = p.op("vector", lambda e: e.nop(), inc=dv)
        c._g_lg_dv = getattr(c, "_g_lg_dv", {})
        c._g_lg_dv[g] = dv_lg

        MX = smx[:, 0:GSZ]
        RS = smx[:, GSZ:2 * GSZ]
        PT = smx[:, 2 * GSZ:3 * GSZ]
        EIX = smx[:, 3 * GSZ:4 * GSZ]
        LRK = smx[:, 4 * GSZ:5 * GSZ]
        lg3 = lg[:, :].rearrange("p (s e) -> p s e", e=8)
        p.fence("vector")
        p.op("vector", lambda e, MX=MX: e.tensor_reduce(out=MX, in_=lg3, axis=AX, op=OP.max))
        p.fence("vector")
        p.op("vector", lambda e, MX=MX: e.tensor_tensor(
            out=ex[:, :].rearrange("p (s e) -> p s e", e=8), in0=lg3,
            in1=MX.unsqueeze(2).broadcast_to((128, GSZ, 8)), op=OP.subtract))
        p.fence("vector")
        dvv = p.op("vector", lambda e: e.nop(), inc=dv)
        p.wait("scalar", dv, dvv)
        p.op("scalar", lambda e: e.activation(out=ex[:, :], in_=ex[:, :], func=AF.Exp))
        p.fence("scalar")
        ac_e = p.op("scalar", lambda e: e.nop(), inc=ac)
        p.wait("vector", ac, ac_e)
        p.op("vector", lambda e, RS=RS: e.tensor_reduce(
            out=RS, in_=ex[:, :].rearrange("p (s e) -> p s e", e=8), axis=AX, op=OP.add))
        p.fence("vector")
        p.op("vector", lambda e, RS=RS: e.reciprocal(out=RS, in_=RS))
        p.fence("vector")
        p.op("vector", lambda e, RS=RS: e.tensor_tensor(
            out=mask[:, :].rearrange("p (s e) -> p s e", e=8),
            in0=ex[:, :].rearrange("p (s e) -> p s e", e=8),
            in1=RS.unsqueeze(2).broadcast_to((128, GSZ, 8)), op=OP.mult))
        p.fence("vector")
        p.op("vector", lambda e, PT=PT: e.tensor_reduce(
            out=PT, in_=mask[:, :].rearrange("p (s e) -> p s e", e=8), axis=AX, op=OP.max))
        p.fence("vector")
        p.op("vector", lambda e, PT=PT: e.tensor_tensor(
            out=mask[:, :].rearrange("p (s e) -> p s e", e=8),
            in0=mask[:, :].rearrange("p (s e) -> p s e", e=8),
            in1=PT.unsqueeze(2).broadcast_to((128, GSZ, 8)), op=OP.is_ge))
        p.fence("vector")
        p.op("vector", lambda e: e.tensor_copy(mask_bf[:, :], mask[:, :]))
        p.op("vector", lambda e: e.tensor_tensor(out=ex[:, :], in0=mask[:, :], in1=eids4[:, :], op=OP.mult))
        p.fence("vector")
        p.op("vector", lambda e, EIX=EIX: e.tensor_reduce(
            out=EIX, in_=ex[:, :].rearrange("p (s e) -> p s e", e=8), axis=AX, op=OP.add))
        p.fence("vector")
        dv_mask = p.op("vector", lambda e: e.nop(), inc=dv)

        p.wait("tensor", dv, dv_mask)
        if g >= 1:
            p.wait("tensor", dv, c._g_rank_dv[g - 1])
        for j in range(GSZ):
            st = g * GSZ + j
            if st > 0:
                p.wait("tensor", ac, off_ac)
            off_mm = p.op("tensor", lambda e, j=j: e.matmul(
                c.pm[0][0:1, 0:8], c.onesc_bf[:, 0:1], mask_bf[:, j * 8:(j + 1) * 8],
                start=True, stop=True), inc=mm)
            p.op("tensor", lambda e, j=j: e.matmul(
                c.ptf[:, j * 8:(j + 1) * 8], c.tri_bf[:, :], mask_bf[:, j * 8:(j + 1) * 8],
                start=True, stop=True))
            p.wait("scalar", mm, off_mm)
            p.op("scalar", lambda e: e.activation(out=cntb[0:1, :], in_=c.pm[0][0:1, 0:8], func=AF.Copy))
            p.fence("scalar")
            off_ac = p.op("scalar", lambda e: e.nop(), inc=ac)
            p.wait("vector", ac, off_ac)
            p.op("vector", lambda e, j=j: e.tensor_copy(offrow[0:1, j * 8:(j + 1) * 8], runrow[0:1, :]))
            p.fence("vector")
            p.op("vector", lambda e: e.tensor_tensor(out=runrow[0:1, :], in0=runrow[0:1, :],
                                                     in1=cntb[0:1, :], op=OP.add))
            p.fence("vector")
            dv_off = p.op("vector", lambda e: e.nop(), inc=dv)
            p.wait("tensor", dv, dv_off)
            p.op("tensor", lambda e, j=j: e.matmul(
                c.pm[1][:, j * 8:(j + 1) * 8], c.ones1_f[0:1, :], offrow[0:1, j * 8:(j + 1) * 8],
                start=True, stop=True), inc=(mm if j == GSZ - 1 else None))
        mm_rank = mm.n
        p.wait("scalar", mm, mm_rank)
        p.op("scalar", lambda e: e.activation(out=offbc[:, :], in_=c.pm[1][:, 0:GSZ * 8], func=AF.Copy))
        p.fence("scalar")
        ac_ob = p.op("scalar", lambda e: e.nop(), inc=ac)
        p.wait("vector", ac, ac_ob)
        p.wait("vector", mm, mm_rank)
        p.op("vector", lambda e: e.scalar_tensor_tensor(
            out=rank_sb[:, :], in0=c.ptf[:, 0:GSZ * 8], scalar=1.0, in1=offbc[:, :],
            op0=OP.mult, op1=OP.add))
        p.fence("vector")
        dv_rank = p.op("vector", lambda e: e.nop(), inc=dv)
        c._g_rank_dv = getattr(c, "_g_rank_dv", {})
        c._g_rank_dv[g] = dv_rank

        QF = tchain[:, 0:GSZ]
        QQ = tchain[:, GSZ:2 * GSZ]
        PM_ = tchain[:, 2 * GSZ:3 * GSZ]
        T1 = tchain[:, 3 * GSZ:4 * GSZ]
        DEST = tchain[:, 4 * GSZ:5 * GSZ]
        OV = tchain[:, 5 * GSZ:6 * GSZ]
        p.op("vector", lambda e: e.tensor_tensor(out=ex[:, :], in0=mask[:, :], in1=rank_sb[:, :], op=OP.mult))
        p.fence("vector")
        p.op("vector", lambda e, LRK=LRK: e.tensor_reduce(
            out=LRK, in_=ex[:, :].rearrange("p (s e) -> p s e", e=8), axis=AX, op=OP.add))
        p.fence("vector")
        p.op("vector", lambda e, QF=QF, LRK=LRK: e.tensor_scalar(
            out=QF, in0=LRK, scalar1=1.0 / 128, scalar2=0.49609375, op0=OP.mult, op1=OP.subtract))
        p.fence("vector")
        p.op("vector", lambda e, QF=QF, QQ=QQ: e.tensor_scalar(
            out=QQ, in0=QF, scalar1=RND, scalar2=RND, op0=OP.add, op1=OP.subtract))
        p.fence("vector")
        p.op("vector", lambda e, QQ=QQ, PM_=PM_, LRK=LRK: e.scalar_tensor_tensor(
            out=PM_, in0=QQ, scalar=-128.0, in1=LRK, op0=OP.mult, op1=OP.add))
        p.fence("vector")
        p.op("vector", lambda e, PM_=PM_, T1=T1, QQ=QQ: e.scalar_tensor_tensor(
            out=T1, in0=PM_, scalar=float(QC), in1=QQ, op0=OP.mult, op1=OP.add))
        p.fence("vector")
        p.op("vector", lambda e, T1=T1, DEST=DEST, EIX=EIX: e.scalar_tensor_tensor(
            out=DEST, in0=EIX, scalar=float(CAP), in1=T1, op0=OP.mult, op1=OP.add))
        p.fence("vector")
        p.op("vector", lambda e, OV=OV, LRK=LRK: e.tensor_scalar(
            out=OV, in0=LRK, scalar1=float(CAP), scalar2=0.0, op0=OP.is_ge, op1=OP.add))
        p.fence("vector")
        p.op("vector", lambda e, OV=OV, DEST=DEST: e.scalar_tensor_tensor(
            out=DEST, in0=OV, scalar=BIG, in1=DEST, op0=OP.mult, op1=OP.add))
        p.fence("vector")
        p.op("vector", lambda e, DEST=DEST: e.tensor_copy(desti[:, :], DEST))
        for j in range(GSZ):
            st = g * GSZ + j
            p.op("vector", lambda e, j=j, st=st: e.tensor_scalar(
                out=rec[:, 2 * j:2 * j + 1], in0=p_iota[:, :], scalar1=1.0,
                scalar2=float(st * 128), op0=OP.mult, op1=OP.add))
            p.op("vector", lambda e, j=j, PT=PT: e.tensor_copy(rec[:, 2 * j + 1:2 * j + 2], PT[:, j:j + 1]))
        p.fence("vector")
        dv_rec = p.op("vector", lambda e: e.nop(), inc=dv)
        c._g_done = getattr(c, "_g_done", {})
        c._g_done[g] = dv_rec
        prev_grp_dv = dv_rec

        p.wait("gpsimd", dv, dv_rec)
        p.wait("gpsimd", sp, p_ld)
        for j in range(GSZ):
            p.op("gpsimd", lambda e, j=j: e.indirect_dma_start(
                out=tabd[:, :],
                out_offset=bass.IndirectOffsetOnAxis(desti[:, j:j + 1], axis=0),
                in_=rec[:, 2 * j:2 * j + 2],
                bounds_check=bndh["r"], oob_is_err=False,
                in_offset=None), inc=so, k=16)


    out = {}
    if L == 1:
        p.wait("gpsimd", dv, dv.n)
        p.op("gpsimd", lambda e: e.dma_start(out=d["cc_in"][:, :], in_=runrow[0:1, :]), inc=scc, k=16)
        p.wait("gpsimd", scc, scc.n)
        p.op("gpsimd", lambda e: e.collective_compute(
            "AllGather", mybir.AluOpType.bypass,
            replica_groups=[list(range(NCORE))],
            ins=[d["cc_in"].opt()], outs=[d["cc_out"].opt()]), inc=scc)
        p.wait("gpsimd", scc, scc.n)
        p.op("gpsimd", lambda e: e.dma_start(out=cc_sb[:, :], in_=d["cc_out"][:, :]), inc=scc, k=16)
        cc_ld = scc.n
        p.wait("tensor", scc, cc_ld)
        mm_off = p.op("tensor", lambda e: e.matmul(
            c.pm[2][0:1, 0:8], c.prefmask[:, :], cc_sb[:, :], start=True, stop=True), inc=mm)
        p.wait("vector", mm, mm_off)
        p.op("vector", lambda e: e.tensor_scalar(
            out=thr_row[0:1, :], in0=c.pm[2][0:1, 0:8], scalar1=-1.0, scalar2=float(CAPG),
            op0=OP.mult, op1=OP.add))
        p.fence("vector")
        dv_thr = p.op("vector", lambda e: e.nop(), inc=dv)
        p.wait("gpsimd", dv, dv_thr)
        p.op("gpsimd", lambda e: e.dma_start(out=d["thr_d"][:, :], in_=thr_row[0:1, :]), inc=scc, k=16)
        p.wait("gpsimd", scc, scc.n)
        p.op("gpsimd", lambda e: e.dma_start(
            out=thr_bc[:, :], in_=d["thr_d"][0:1, :].broadcast_to((128, 8))), inc=scc, k=16)
        p.wait("gpsimd", scc, scc.n)
        out["thr_bc"] = thr_bc

    jn = emit_join(c, extra=[(so, so.n), (sox[0], sox[0].n), (sox[1], sox[1].n),
                             (sx[0], sx[0].n), (sx[1], sx[1].n), (spw, spw.n),
                             (sp, sp.n)] + ([(scc, scc.n)] if L == 1 else []))
    c.gate_join = jn
    ph.close()
    return out

NST = CAP // 512
PASSES = [(0, 2), (2, 4)] if NST == 4 else ([(0, 2), (2, 3)] if NST == 3 else [(0, 2), (2, 4), (4, 5)])


def emit_expert(c, layer, thr_bc=None):
    nc, p, stack, d = c.nc, c.p, c.stack, c.dram
    ph = ExitStack()
    L = layer
    xsrc = d["xB"] if L == 0 else d["xD"]
    xlnd = d["xlnB"] if L == 0 else d["xlnD"]
    tabd = d["tabB"] if L == 0 else d["tabD"]
    xdst = d["xC"] if L == 0 else d["x_out"]
    w1n = "l0_w1" if L == 0 else "l1_w1"
    w2n = "l0_w2" if L == 0 else "l1_w2"
    pfx = f"e{L}_"
    sb = lambda n, shp, dt: ph.enter_context(nc.sbuf_tensor(pfx + n, shp, dt))

    tabsb = sb("tab", [128, QC * 2], F32)
    idxi = sb("idxi", [128, QC], I32)
    pe_eff = sb("peeff", [128, QC], F32)
    xgs = [sb(f"xgs{i}", [128, 4 * D], F32) for i in range(2)]
    xlg = sb("xlg", [128, 4 * D], BF16)
    xlnT = [sb(f"xlnT{i}", [128, DCH * 512], BF16) for i in range(2)]
    hsb = [sb(f"h{i}", [128, HCH * 512], BF16) for i in range(2)]
    out_t = [sb(f"ot{i}", [128, 4 * D], F32) for i in range(2)]
    oTc = [sb(f"oTc{i}", [128, 512], F32) for i in range(2)]
    wb = [sb(f"wb{i}", [128, 8 * 512], BF16) for i in range(2)]
    b1t = sb("b1t", [128, NEXP * HCH], F32) if L == 0 else None
    b2t = sb("b2t", [128, NEXP * DCH], F32) if L == 0 else None
    kq = sb("kq", [128, QC], F32) if L == 1 else None
    if getattr(c, "edbg", None) is not None and "edbg_o" in c.edbg:
        c.edbg_osb = sb("edbgosb", [128, 4 * 512], F32)

    sg = Sem(nc, stack, pfx + "sg")
    sw = [Sem(nc, stack, pfx + "sw0"), Sem(nc, stack, pfx + "sw1")]
    sp = Sem(nc, stack, pfx + "sp")
    mm = Sem(nc, stack, pfx + "mm")
    ac = Sem(nc, stack, pfx + "ac")
    dv = Sem(nc, stack, pfx + "dv")
    so = Sem(nc, stack, pfx + "so")

    jv = c.join.n
    for eng in ENG_ORDER:
        p.wait(eng, c.join, jv)
        p.wait(eng, c.s_init, c.init_target)
    bndh = {}
    def _init_bnd(e):
        r = e.alloc_register(pfx + "bnd")
        e.reg_mov(r, TOK - 1)
        bndh["r"] = r
    p.q["gpsimd"].append(_init_bnd)

    if L == 0:
        p.op("sync", lambda e: e.dma_start(out=b1t[:, :], in_=d["l0_b1_t"][:, :]), inc=sp, k=16)
        p.op("sync", lambda e: e.dma_start(out=b2t[:, :], in_=d["l0_b2_t"][:, :]), inc=sp, k=16)
        p_ld = sp.n
        p.wait("scalar", sp, p_ld)

    bank_free = [None] * 4
    tb_free = {"ptA": None, "ptB": None, "ptf": None, "pg": None}
    oTc_free = [None, None]
    xlnT_free = [None, None]
    hs_free = [None, None]
    slab_prev = {}
    g_ld = {}
    out_dv = {}
    xg_free = {}
    out_t_free = {}
    xlg_mm = 0
    last_sc = 0
    prev_idx_dv = 0

    def stile(sti):
        return sti % 2

    nexp_emit = getattr(c, 'only_expert', NEXP)
    passes_emit = getattr(c, 'only_passes', PASSES)
    for ee in range(nexp_emit):
        if ee >= 1:
            p.wait("sync", dv, prev_idx_dv)
        p.op("sync", lambda e, ee=ee: e.dma_start(
            out=tabsb[:, :],
            in_=tabd[:, :].rearrange("a b -> (a b)")[2 * CAP * ee:2 * CAP * (ee + 1)]
                .rearrange("(p g) -> p g", p=128)), inc=sp, k=16)
        tab_ld = sp.n
        p.wait("vector", sp, tab_ld)
        if ee >= 1:
            p.wait("vector", so, last_sc)
        p.op("vector", lambda e: e.tensor_copy(
            idxi[:, :], tabsb[:, :].rearrange("p (q g) -> p q g", g=2)[:, :, 0]))
        if L == 1:
            p.op("vector", lambda e, ee=ee: e.tensor_tensor(
                out=kq[:, :], in0=c.slot_iota[:, :],
                in1=thr_bc[:, ee:ee + 1].broadcast_to((128, QC)), op=OP.is_lt))
            p.fence("vector")
            p.op("vector", lambda e: e.tensor_tensor(
                out=pe_eff[:, :], in0=kq[:, :],
                in1=tabsb[:, :].rearrange("p (q g) -> p q g", g=2)[:, :, 1], op=OP.mult))
        else:
            p.op("vector", lambda e: e.tensor_copy(
                pe_eff[:, :], tabsb[:, :].rearrange("p (q g) -> p q g", g=2)[:, :, 1]))
        p.fence("vector")
        prev_idx_dv = p.op("vector", lambda e: e.nop(), inc=dv)

        for (st0, st1) in passes_emit:
            S = list(range(st0, st1))
            for sti in S:
                b = stile(sti)
                p.wait("gpsimd", dv, prev_idx_dv)
                if xg_free.get(b) is not None:
                    p.wait("gpsimd", dv, xg_free[b])
                p.wait("gpsimd", mm, xlg_mm)
                for q4 in range(4):
                    q = 4 * sti + q4
                    p.op("gpsimd", lambda e, q=q, q4=q4, b=b: e.indirect_dma_start(
                        out=xgs[b][:, cs(q4, D)], out_offset=None,
                        in_=xsrc[:, :],
                        in_offset=bass.IndirectOffsetOnAxis(idxi[:, q:q + 1], axis=0),
                        bounds_check=bndh["r"], oob_is_err=False), inc=sg, k=16)
                    p.op("gpsimd", lambda e, q=q, q4=q4: e.indirect_dma_start(
                        out=xlg[:, cs(q4, D)], out_offset=None,
                        in_=xlnd[:, :],
                        in_offset=bass.IndirectOffsetOnAxis(idxi[:, q:q + 1], axis=0),
                        bounds_check=bndh["r"], oob_is_err=False), inc=sg, k=16)
                g_ld[sti] = sg.n
                p.wait("tensor", sg, g_ld[sti])
                for dc in range(DCH):
                    key = "ptA" if dc % 2 == 0 else "ptB"
                    bank = c.ptA if dc % 2 == 0 else c.ptB
                    if tb_free[key] is not None:
                        p.wait("tensor", *tb_free[key])
                    for q4 in range(4):
                        p.op("tensor", lambda e, dc=dc, q4=q4, bank=bank: e.transpose(
                            bank[:, cs(q4, 128)], xlg[:, cs(q4 * DCH + dc, 128)], c.ident_bf[:, :]),
                            inc=(mm if q4 == 3 else None))
                    mv = mm.n
                    p.wait("vector", mm, mv)
                    if dc == 0 and xlnT_free[b] is not None:
                        p.wait("vector", mm, xlnT_free[b])
                    ev = p.op("vector", lambda e, dc=dc, b=b, bank=bank: e.tensor_copy(
                        xlnT[b][:, cs(dc, 512)], bank[:, :]), inc=dv)
                    tb_free[key] = (dv, ev)
                xlg_mm = mm.n
                p.fence("vector")
                p.op("vector", lambda e: e.nop(), inc=dv)
                p.wait("tensor", dv, dv.n)

            first_h_evac = True
            for hg in range(HCH // 4):
                bufp = hg % 2
                buf = wb[bufp]
                for half in (0, 1):
                    if slab_prev.get((bufp, half)) is not None:
                        p.wait("gpsimd", mm, slab_prev[(bufp, half)])
                swp = sw[bufp]
                p.op("gpsimd", lambda e, ee=ee, hg=hg, buf=buf: e.dma_start(
                    out=buf[:, :].rearrange("p (k q) -> p k q", q=512),
                    in_=d[w1n][ee, :, cs(hg, 512)].rearrange("(k p) q -> p k q", p=128)), inc=swp, k=16)
                p.wait("tensor", swp, swp.n)
                for hc4 in range(4):
                    hc = hg * 4 + hc4
                    for sti in S:
                        j = (hc4 * len(S) + (sti - st0)) % 4
                        bk = c.pm[j]
                        if bank_free[j] is not None:
                            p.wait("tensor", *bank_free[j])
                        for k in range(DCH):
                            p.op("tensor", lambda e, k=k, hc4=hc4, bk=bk, buf=buf, sti=sti: e.matmul(
                                bk[:, :], buf[:, cs(k, 512)][:, cs(hc4, 128)],
                                xlnT[stile(sti)][:, cs(k, 512)],
                                start=(k == 0), stop=(k == DCH - 1)),
                                inc=(mm if k == DCH - 1 else None))
                        mv = mm.n
                        p.wait("scalar", mm, mv)
                        if first_h_evac:
                            first_h_evac = False
                            for b2 in set(stile(s) for s in S):
                                if hs_free[b2] is not None:
                                    p.wait("scalar", mm, hs_free[b2])
                        if L == 0:
                            aev = p.op("scalar", lambda e, ee=ee, hc=hc, sti=sti, bk=bk: e.activation(
                                out=hsb[stile(sti)][:, cs(hc, 512)], in_=bk[:, :], func=AF.Silu,
                                bias=b1t[:, ee * HCH + hc:ee * HCH + hc + 1]), inc=ac)
                        else:
                            aev = p.op("scalar", lambda e, hc=hc, sti=sti, bk=bk: e.activation(
                                out=hsb[stile(sti)][:, cs(hc, 512)], in_=bk[:, :], func=AF.Relu), inc=ac)
                        bank_free[j] = (ac, aev)
                slab_prev[(bufp, 0)] = mm.n
                slab_prev[(bufp, 1)] = mm.n
            p.fence("scalar")
            ac_h = p.op("scalar", lambda e: e.nop(), inc=ac)
            xlnT_free_val = mm.n
            for sti in S:
                xlnT_free[stile(sti)] = xlnT_free_val

            p.wait("tensor", ac, ac_h)
            for dg in range(4):
                for kq4 in range(4):
                    bufp = kq4 % 2
                    half = kq4 // 2
                    swp = sw[bufp]
                    if slab_prev.get((bufp, half)) is not None:
                        p.wait("gpsimd", mm, slab_prev[(bufp, half)])
                    p.op("gpsimd", lambda e, ee=ee, dg=dg, kq4=kq4, bufp=bufp, half=half: e.dma_start(
                        out=wb[bufp][:, half * 2048:(half + 1) * 2048].rearrange("p (k q) -> p k q", q=256),
                        in_=d[w2n][ee, kq4 * 1024:(kq4 + 1) * 1024, cs(dg, 256)]
                            .rearrange("(k p) q -> p k q", p=128)), inc=swp, k=16)
                p.wait("tensor", sw[0], sw[0].n)
                p.wait("tensor", sw[1], sw[1].n)
                for dc2 in range(2):
                    for sti in S:
                        j = dc2 * 2 + (sti - st0)
                        bk = c.pm[j]
                        if bank_free[j] is not None:
                            p.wait("tensor", *bank_free[j])
                        for kk in range(HCH):
                            kq4 = kk // 8
                            k8 = kk % 8
                            bufp = kq4 % 2
                            half = kq4 // 2
                            p.op("tensor", lambda e, k8=k8, kk=kk, dc2=dc2, bk=bk, bufp=bufp, half=half, sti=sti: e.matmul(
                                bk[:, :], wb[bufp][:, half * 2048 + k8 * 256 + dc2 * 128:half * 2048 + k8 * 256 + dc2 * 128 + 128],
                                hsb[stile(sti)][:, cs(kk, 512)],
                                start=(kk == 0), stop=(kk == HCH - 1)),
                                inc=(mm if kk == HCH - 1 else None))
                for kq4 in range(4):
                    slab_prev[(kq4 % 2, kq4 // 2)] = mm.n
                mm_dg = mm.n
                if ee == 0 and st0 == 0 and getattr(c, "edbg", None) is not None and "edbg_o" in c.edbg:
                    p.wait("scalar", mm, mm_dg)
                    p.op("scalar", lambda e, dg=dg: e.activation(
                        out=c.edbg_osb[:, cs(dg, 512)], in_=c.pm[0][:, :], func=AF.Copy))
                    p.fence("scalar")
                    aevd = p.op("scalar", lambda e: e.nop(), inc=ac)
                    bank_free[0] = (ac, aevd)
                if dg == 0 and ee == 0 and st0 == 0 and getattr(c, "edbg", None) is not None and "edbg_wb" in c.edbg:
                    p.wait("gpsimd", mm, mm_dg)
                    p.op("gpsimd", lambda e: e.dma_start(out=c.edbg["edbg_wb"][:, 0:2048], in_=wb[0][:, 0:2048]), inc=so, k=16)
                    p.op("gpsimd", lambda e: e.dma_start(out=c.edbg["edbg_wb"][:, 2048:4096], in_=wb[1][:, 0:2048]), inc=so, k=16)
                for dc2 in range(2):
                    dc = dg * 2 + dc2
                    for sti in S:
                        j = dc2 * 2 + (sti - st0)
                        bk = c.pm[j]
                        b = stile(sti)
                        pp = (dc * len(S) + (sti - st0)) % 2
                        p.wait("scalar", mm, mm_dg)
                        if oTc_free[pp] is not None:
                            p.wait("scalar", mm, oTc_free[pp])
                        if L == 0:
                            p.op("scalar", lambda e, ee=ee, dc=dc, pp=pp, bk=bk: e.activation(
                                out=oTc[pp][:, :], in_=bk[:, :], func=AF.Identity,
                                bias=b2t[:, ee * DCH + dc:ee * DCH + dc + 1]))
                        else:
                            p.op("scalar", lambda e, pp=pp, bk=bk: e.activation(
                                out=oTc[pp][:, :], in_=bk[:, :], func=AF.Copy))
                        p.fence("scalar")
                        aev = p.op("scalar", lambda e: e.nop(), inc=ac)
                        bank_free[j] = (ac, aev)
                        key = "ptf" if dc % 2 == 0 else "pg"
                        tb = c.ptf if dc % 2 == 0 else c.pg
                        p.wait("tensor", ac, aev)
                        if tb_free[key] is not None:
                            p.wait("tensor", *tb_free[key])
                        for q4 in range(4):
                            p.op("tensor", lambda e, q4=q4, pp=pp, tb=tb: e.transpose(
                                tb[:, cs(q4, 128)], oTc[pp][:, cs(q4, 128)], c.ident_f[:, :]),
                                inc=(mm if q4 == 3 else None))
                        mtv = mm.n
                        oTc_free[pp] = mtv
                        p.wait("vector", mm, mtv)
                        p.wait("vector", sg, g_ld[sti])
                        if dc == 0 and out_t_free.get(b) is not None:
                            p.wait("vector", so, out_t_free[b])
                        for q4 in range(4):
                            q = 4 * sti + q4
                            p.op("vector", lambda e, q=q, q4=q4, dc=dc, b=b, tb=tb: e.scalar_tensor_tensor(
                                out=out_t[b][:, q4 * D + dc * 128:q4 * D + dc * 128 + 128],
                                in0=tb[:, cs(q4, 128)], scalar=pe_eff[:, q:q + 1],
                                in1=xgs[b][:, q4 * D + dc * 128:q4 * D + dc * 128 + 128],
                                op0=OP.mult, op1=OP.add),
                                inc=(dv if q4 == 3 else None))
                        dvv = dv.n
                        tb_free[key] = (dv, dvv)
                        out_dv[(sti, dc)] = dvv
                hs_free_val = mm_dg
            for b2 in set(stile(s) for s in S):
                hs_free[b2] = mm.n

            if ee == 0 and st0 == 0 and getattr(c, "edbg", None) is not None:
                p.wait("gpsimd", dv, dv.n)
                p.wait("gpsimd", ac, ac.n)
                if "edbg_o" in c.edbg:
                    p.op("gpsimd", lambda e: e.dma_start(out=c.edbg["edbg_o"][:, :], in_=c.edbg_osb[:, :]), inc=so, k=16)
                for nm, buf in [("edbg_xlnT", xlnT[0]), ("edbg_h", hsb[0]), ("edbg_out", out_t[0]),
                                ("edbg_xg", xgs[0]), ("edbg_pe", pe_eff), ("edbg_tab", tabsb),
                                ("edbg_idxi", idxi), ("edbg_xlg", xlg)]:
                    if nm in c.edbg:
                        p.op("gpsimd", lambda e, nm=nm, buf=buf: e.dma_start(
                            out=c.edbg[nm][:, :], in_=buf[:, :]), inc=so, k=16)
            for sti in S:
                b = stile(sti)
                p.wait("gpsimd", dv, out_dv[(sti, DCH - 1)])
                for q4 in range(4):
                    q = 4 * sti + q4
                    p.op("gpsimd", lambda e, q=q, q4=q4, b=b: e.indirect_dma_start(
                        out=xdst[:, :],
                        out_offset=bass.IndirectOffsetOnAxis(idxi[:, q:q + 1], axis=0),
                        in_=out_t[b][:, cs(q4, D)], in_offset=None,
                        bounds_check=bndh["r"], oob_is_err=False), inc=so, k=16)
                xg_free[b] = out_dv[(sti, DCH - 1)]
                out_t_free[b] = so.n
            last_sc = so.n

    jn = emit_join(c, extra=[(so, so.n), (sg, sg.n), (sw[0], sw[0].n), (sw[1], sw[1].n), (sp, sp.n)])
    c.exp_join = jn
    ph.close()

def emit_l1_attn(c):
    nc, p, stack, d = c.nc, c.p, c.stack, c.dram
    ph = ExitStack()
    sb = lambda n, shp, dt: ph.enter_context(nc.sbuf_tensor("b_" + n, shp, dt))
    xi = [sb(f"xi{i}", [128, 4 * D], F32) for i in range(2)]
    xln = sb("xln", [128, 4 * D], BF16)
    xlnT = sb("xlnT", [128, DCH * TT], BF16)
    res_f = sb("resf", [128, DCH * TT], F32)
    uc = sb("uc", [128, ECH * TT], F32)
    ucb = sb("ucb", [128, ECH * TT], BF16)
    sgd = sb("sgd", [128, 4 * TT], F32)
    xpt = [sb(f"xpt{i}", [128, TT + 3], F32) for i in range(2)]
    xcf = [sb(f"xcf{i}", [128, TT], F32) for i in range(2)]
    oTc = [sb(f"oTc{i}", [128, TT], F32) for i in range(2)]
    halo = sb("halo", [128, 3 * ECH], F32)
    carry = sb("carry", [128, ECH], F32)
    dec9 = sb("dec9", [128, 1], F32)
    musum = [sb(f"mus{i}", [128, 4], F32) for i in range(2)]
    m2sum = [sb(f"m2s{i}", [128, 4], F32) for i in range(2)]
    lnt = [sb(f"lnt{i}", [128, 16], F32) for i in range(2)]
    wb = [sb(f"wb{i}", [128, 16 * 512], BF16) for i in range(2)]
    convw = sb("convw", [128, 4 * ECH], F32)
    convb = sb("convb", [128, ECH], F32)
    ndt_b = sb("ndtb", [128, ECH], F32)
    sx = [Sem(nc, stack, "b_sx0"), Sem(nc, stack, "b_sx1")]
    sw = [Sem(nc, stack, "b_sw0"), Sem(nc, stack, "b_sw1")]
    sp = Sem(nc, stack, "b_sp")
    mm = Sem(nc, stack, "b_mm")
    ac = Sem(nc, stack, "b_ac")
    dv = Sem(nc, stack, "b_dv")
    so = [Sem(nc, stack, "b_so0"), Sem(nc, stack, "b_so1")]

    jv = c.join.n
    for eng in ENG_ORDER:
        p.wait(eng, c.join, jv)
        p.wait(eng, c.s_init, c.init_target)

    for name, dst in [("l1_convw", convw), ("l1_convb", convb), ("l1_dtb_t", ndt_b)]:
        p.op("sync", lambda e, n=name, t=dst: e.dma_start(out=t[:, :], in_=d[n][:, :]), inc=sp, k=16)
    p_ld = sp.n
    p.wait("vector", sp, p_ld)
    p.op("vector", lambda e: e.tensor_scalar(out=ndt_b[:, :], in0=ndt_b[:, :], scalar1=-1.0,
                                             scalar2=0.0, op0=OP.mult, op1=OP.add))
    p.op("vector", lambda e: e.memset(dec9[:, :], 0.9))
    p.op("vector", lambda e: e.memset(halo[:, :], 0.0))
    dv_init = p.op("vector", lambda e: e.memset(carry[:, :], 0.0), inc=dv)
    p.fence("vector")

    stages = ([("l1_inproj_t", og, 8, "win") for og in range(6)]
              + [("l1_dtproj_t", og, 16, "wd") for og in range(4)]
              + [("l1_outproj_t", og, 16, "wout") for og in range(2)])
    NS = len(stages)

    stage_done_mm = {}
    stage_sw = {}
    bank_free = [None] * 4
    tb_free = {"ptA": None, "ptB": None, "ptf": None, "pg": None}
    oTc_free = [None, None]
    xr_done = {}
    store_done = {}
    mm_T = {}
    pending_store = None

    def load_stage(i, s_i):
        gidx = i * NS + s_i
        wname, og, nk, kind = stages[s_i]
        if gidx >= 2 and (gidx - 2) in stage_done_mm:
            p.wait("gpsimd", mm, stage_done_mm[gidx - 2])
        buf = wb[gidx % 2]
        swp = sw[gidx % 2]
        p.op("gpsimd", lambda e, wname=wname, og=og, nk=nk, buf=buf: e.dma_start(
            out=buf[:, 0:nk * 512].rearrange("p (k q) -> p k q", q=512),
            in_=d[wname][0:nk * 128, cs(og, 512)].rearrange("(k p) q -> p k q", p=128)), inc=swp, k=16)
        stage_sw[gidx] = (swp, swp.n)

    for i in range(NTILE):
        par = i % 2
        xib = xi[par]
        mus, m2s, lt = musum[par], m2sum[par], lnt[par]
        fob = (i % TPB == 0)
        MU = lt[:, 0:4]
        VE = lt[:, 4:8]
        R1 = lt[:, 8:12]

        if i >= 2:
            p.wait("sync", *store_done[i - 2])
        sxp = sx[par]
        p.op("sync", lambda e, i=i, xib=xib: e.dma_start(
            out=xib[:, :].rearrange("p (s q) -> p s q", s=4),
            in_=d["xC"][i * TT:(i + 1) * TT, :].rearrange("(s p) q -> p s q", p=128)), inc=sxp, k=16)
        x_ld = (sxp, sxp.n)
        load_stage(i, 0)
        load_stage(i, 1)

        p.wait("vector", *x_ld)
        p.wait("scalar", *x_ld)
        if i == 0:
            p.wait("vector", dv, dv_init)
        if i >= 1:
            p.wait("vector", mm, mm_T[i - 1])
        for s in range(4):
            p.op("vector", lambda e, s=s, xib=xib, mus=mus: e.tensor_reduce(
                out=mus[:, s:s + 1], in_=xib[:, cs(s, D)], axis=AX, op=OP.add))
            p.op("scalar", lambda e, s=s, xib=xib, m2s=m2s: e.activation(
                out=uc[:, 0:D], in_=xib[:, cs(s, D)], func=AF.Square,
                accum_out=m2s[:, s:s + 1]), inc=ac)
        ac_sq = ac.n
        if i >= 1:
            p.wait("scalar", dv, xr_done.get(i - 1, 0) or 0)
        p.fence("vector")
        p.op("vector", lambda e, mus=mus, MU=MU: e.tensor_scalar(
            out=MU, in0=mus[:, :], scalar1=1.0 / D, scalar2=0.0, op0=OP.mult, op1=OP.add))
        p.fence("vector")
        p.op("vector", lambda e, MU=MU, VE=VE: e.tensor_tensor(out=VE, in0=MU, in1=MU, op=OP.mult))
        p.fence("vector")
        p.wait("vector", ac, ac_sq)
        p.op("vector", lambda e, m2s=m2s, VE=VE: e.scalar_tensor_tensor(
            out=VE, in0=m2s[:, :], scalar=1.0 / D, in1=VE, op0=OP.mult, op1=OP.subtract))
        p.fence("vector")
        p.op("vector", lambda e, VE=VE: e.tensor_scalar(
            out=VE, in0=VE, scalar1=EPS, scalar2=0.0, op0=OP.add, op1=OP.add))
        p.fence("vector")
        dvv = p.op("vector", lambda e: e.nop(), inc=dv)
        p.wait("scalar", dv, dvv)
        p.op("scalar", lambda e, VE=VE, R1=R1: e.activation(out=R1, in_=VE, func=AF.Ln))
        p.fence("scalar")
        p.op("scalar", lambda e, R1=R1: e.activation(out=R1, in_=R1, func=AF.Exp, scale=-0.5))
        p.fence("scalar")
        ac_r = p.op("scalar", lambda e: e.nop(), inc=ac)
        p.wait("vector", ac, ac_r)
        for s in range(4):
            p.op("vector", lambda e, s=s, xib=xib, MU=MU, R1=R1: e.tensor_scalar(
                out=xln[:, cs(s, D)], in0=xib[:, cs(s, D)], scalar1=MU[:, s:s + 1],
                scalar2=R1[:, s:s + 1], op0=OP.subtract, op1=OP.mult))
        p.fence("vector")
        dv_xln = p.op("vector", lambda e: e.nop(), inc=dv)

        p.wait("tensor", dv, dv_xln)
        for dc in range(DCH):
            key = "ptA" if dc % 2 == 0 else "ptB"
            bank = c.ptA if dc % 2 == 0 else c.ptB
            if tb_free[key] is not None:
                p.wait("tensor", *tb_free[key])
            for s in range(4):
                p.op("tensor", lambda e, dc=dc, s=s, bank=bank: e.transpose(
                    bank[:, cs(s, 128)], xln[:, cs(s * DCH + dc, 128)], c.ident_bf[:, :]),
                    inc=(mm if s == 3 else None))
            mv = mm.n
            p.wait("vector", mm, mv)
            ev = p.op("vector", lambda e, dc=dc, bank=bank: e.tensor_copy(
                xlnT[:, cs(dc, TT)], bank[:, :]), inc=dv)
            tb_free[key] = (dv, ev)
        mm_T[i] = mm.n
        p.fence("vector")
        dv_xlnT = p.op("vector", lambda e: e.nop(), inc=dv)

        conv_dv = {}
        cast_ac = {}
        g01_dv = {}
        for s_i in range(NS):
            wname, og, nk, kind = stages[s_i]
            gidx = i * NS + s_i
            buf = wb[gidx % 2]
            p.wait("tensor", *stage_sw[gidx])
            if kind == "win":
                p.wait("tensor", dv, dv_xlnT)
                rhs = xlnT
            elif kind == "wd":
                if s_i == 6:
                    p.fence("scalar")
                    ac_ucb = p.op("scalar", lambda e: e.nop(), inc=ac)
                    p.wait("tensor", ac, ac_ucb)
                rhs = ucb
            else:
                if s_i == 10:
                    p.fence("scalar")
                    ac_yb = p.op("scalar", lambda e: e.nop(), inc=ac)
                    p.wait("tensor", ac, ac_yb)
                rhs = ucb
            for j in range(4):
                och = og * 4 + j
                bk = c.pm[j]
                if bank_free[j] is not None:
                    p.wait("tensor", *bank_free[j])
                for k in range(nk):
                    p.op("tensor", lambda e, k=k, j=j, bk=bk, buf=buf, rhs=rhs, nk=nk: e.matmul(
                        bk[:, :], buf[:, cs(k, 512)][:, cs(j, 128)], rhs[:, cs(k, TT)],
                        start=(k == 0), stop=(k == nk - 1)),
                        inc=(mm if k == nk - 1 else None))
                mv = mm.n
                if kind == "win" and och < ECH:
                    pp = och % 2
                    if och >= 2:
                        p.wait("scalar", dv, conv_dv[och - 2])
                    p.wait("scalar", mm, mv)
                    aev = p.op("scalar", lambda e, pp=pp, bk=bk: e.activation(
                        out=xpt[pp][:, 3:3 + TT], in_=bk[:, :], func=AF.Copy), inc=ac)
                    bank_free[j] = (ac, aev)
                    p.wait("vector", ac, aev)
                    if och >= 2:
                        p.wait("vector", ac, cast_ac[och - 2])
                    if fob:
                        p.op("vector", lambda e, pp=pp: e.memset(xpt[pp][:, 0:3], 0.0))
                    else:
                        p.op("vector", lambda e, pp=pp, och=och: e.tensor_copy(
                            xpt[pp][:, 0:3], halo[:, och * 3:och * 3 + 3]))
                    p.op("vector", lambda e, och=och, pp=pp: e.tensor_scalar(
                        out=xcf[pp][:, :], in0=xpt[pp][:, 3:3 + TT],
                        scalar1=convw[:, och * 4 + 3:och * 4 + 4], scalar2=convb[:, och:och + 1],
                        op0=OP.mult, op1=OP.add))
                    for kk in range(3):
                        p.op("vector", lambda e, och=och, pp=pp, kk=kk: e.scalar_tensor_tensor(
                            out=xcf[pp][:, :], in0=xpt[pp][:, kk:kk + TT],
                            scalar=convw[:, och * 4 + kk:och * 4 + kk + 1], in1=xcf[pp][:, :],
                            op0=OP.mult, op1=OP.add))
                    p.op("vector", lambda e, och=och, pp=pp: e.tensor_copy(
                        uc[:, cs(och, TT)], xcf[pp][:, :]))
                    cdv = p.op("vector", lambda e, pp=pp, och=och: e.tensor_copy(
                        halo[:, och * 3:och * 3 + 3], xpt[pp][:, TT:TT + 3]), inc=dv)
                    conv_dv[och] = cdv
                    p.wait("scalar", dv, cdv)
                    cst = p.op("scalar", lambda e, och=och: e.activation(
                        out=ucb[:, cs(och, TT)], in_=uc[:, cs(och, TT)], func=AF.Copy), inc=ac)
                    cast_ac[och] = cst
                elif kind == "win":
                    rch = och - ECH
                    p.wait("scalar", mm, mv)
                    aev = p.op("scalar", lambda e, rch=rch, bk=bk: e.activation(
                        out=res_f[:, cs(rch, TT)], in_=bk[:, :], func=AF.Copy), inc=ac)
                    bank_free[j] = (ac, aev)
                elif kind == "wd":
                    p.wait("scalar", mm, mv)
                    if och >= 4:
                        p.wait("scalar", dv, g01_dv[och - 4])
                    aev = p.op("scalar", lambda e, j=j, och=och, bk=bk: e.activation(
                        out=sgd[:, cs(j, TT)], in_=bk[:, :], func=AF.Sigmoid, scale=-1.0,
                        bias=ndt_b[:, och:och + 1]), inc=ac)
                    bank_free[j] = (ac, aev)
                    p.wait("vector", ac, aev)
                    p.op("vector", lambda e, j=j: e.tensor_scalar(
                        out=sgd[:, cs(j, TT)], in0=sgd[:, cs(j, TT)], scalar1=10.0, scalar2=10.0,
                        op0=OP.mult, op1=OP.add))
                    p.op("vector", lambda e, j=j: e.reciprocal(
                        out=sgd[:, cs(j, TT)], in_=sgd[:, cs(j, TT)]))
                    gdv = p.op("vector", lambda e, j=j, och=och: e.tensor_tensor(
                        out=uc[:, cs(och, TT)], in0=uc[:, cs(och, TT)], in1=sgd[:, cs(j, TT)],
                        op=OP.mult), inc=dv)
                    g01_dv[och] = gdv
                else:
                    pp = j % 2
                    dc = och
                    p.wait("scalar", mm, mv)
                    if oTc_free[pp] is not None:
                        p.wait("scalar", mm, oTc_free[pp])
                    aev = p.op("scalar", lambda e, pp=pp, bk=bk: e.activation(
                        out=oTc[pp][:, :], in_=bk[:, :], func=AF.Copy), inc=ac)
                    bank_free[j] = (ac, aev)
                    p.wait("vector", ac, aev)
                    p.op("vector", lambda e, pp=pp, dc=dc: e.tensor_tensor(
                        out=oTc[pp][:, :], in0=oTc[pp][:, :], in1=res_f[:, cs(dc, TT)], op=OP.add))
                    p.fence("vector")
                    dv_oT = p.op("vector", lambda e: e.nop(), inc=dv)
                    key = "ptf" if dc % 2 == 0 else "pg"
                    tb = c.ptf if dc % 2 == 0 else c.pg
                    p.wait("tensor", dv, dv_oT)
                    if tb_free[key] is not None:
                        p.wait("tensor", *tb_free[key])
                    for q in range(4):
                        p.op("tensor", lambda e, q=q, pp=pp, tb=tb: e.transpose(
                            tb[:, cs(q, 128)], oTc[pp][:, cs(q, 128)], c.ident_f[:, :]),
                            inc=(mm if q == 3 else None))
                    mtv = mm.n
                    oTc_free[pp] = mtv
                    p.wait("vector", mm, mtv)
                    dvv = p.op("vector", lambda e, dc=dc, tb=tb, xib=xib: e.tensor_tensor(
                        out=xib[:, :].rearrange("p (s t q) -> p s t q", s=4, t=DCH)[:, :, dc, :],
                        in0=tb[:, :].rearrange("p (s q) -> p s q", s=4),
                        in1=xib[:, :].rearrange("p (s t q) -> p s t q", s=4, t=DCH)[:, :, dc, :],
                        op=OP.add), inc=dv)
                    tb_free[key] = (dv, dvv)
            stage_done_mm[gidx] = mm.n
            if s_i + 2 < NS:
                load_stage(i, s_i + 2)

            if s_i == 9:
                if fob:
                    p.op("vector", lambda e: e.memset(carry[:, :], 0.0))
                    p.fence("vector")
                dt_done = mm.n
                p.wait("vector", mm, dt_done)
                for chn in range(ECH):
                    p.wait("vector", dv, g01_dv[chn])
                    p.op("vector", lambda e, chn=chn: e.tensor_tensor_scan(
                        out=uc[:, cs(chn, TT)],
                        data0=dec9[:, 0:1].broadcast_to((128, TT)),
                        data1=uc[:, cs(chn, TT)],
                        initial=carry[:, chn:chn + 1],
                        op0=OP.mult, op1=OP.add))
                    p.op("vector", lambda e, chn=chn: e.tensor_copy(
                        carry[:, chn:chn + 1], uc[:, chn * TT + TT - 1:chn * TT + TT]))
                p.fence("vector")
                dv_s = p.op("vector", lambda e: e.nop(), inc=dv)
                p.wait("scalar", dv, dv_s)
                p.wait("scalar", mm, dt_done)
                for chn in range(ECH):
                    p.op("scalar", lambda e, chn=chn: e.activation(
                        out=ucb[:, cs(chn, TT)], in_=uc[:, cs(chn, TT)], func=AF.Copy))

        p.fence("vector")
        xr_done[i] = p.op("vector", lambda e: e.nop(), inc=dv)
        if pending_store is not None:
            pass
        pending_store = (i, xib)
        if i > 0:
            pass
        if i >= 1:
            pj, pxib = i - 1, xi[(i - 1) % 2]
            sop = so[pj % 2]
            p.wait("scalar", dv, xr_done[pj])
            p.op("scalar", lambda e, pj=pj, pxib=pxib: e.dma_start(
                out=d["xD"][pj * TT:(pj + 1) * TT, :].rearrange("(s p) q -> p s q", p=128),
                in_=pxib[:, :].rearrange("p (s q) -> p s q", s=4)), inc=sop, k=16)
            store_done[pj] = (sop, sop.n)

    pj = NTILE - 1
    pxib = xi[pj % 2]
    sop = so[pj % 2]
    p.wait("scalar", dv, xr_done[pj])
    p.op("scalar", lambda e, pj=pj, pxib=pxib: e.dma_start(
        out=d["xD"][pj * TT:(pj + 1) * TT, :].rearrange("(s p) q -> p s q", p=128),
        in_=pxib[:, :].rearrange("p (s q) -> p s q", s=4)), inc=sop, k=16)
    store_done[pj] = (sop, sop.n)
    emit_join(c, extra=[(so[0], so[0].n), (so[1], so[1].n), (sx[0], sx[0].n), (sx[1], sx[1].n),
                        (sw[0], sw[0].n), (sw[1], sw[1].n), (sp, sp.n)])
    ph.close()

def _t(a):
    return np.ascontiguousarray(np.asarray(a).T).astype(np.float32, copy=False)


def _chunkcol(v, nch):
    return np.ascontiguousarray(np.asarray(v).reshape(nch, 128).T).astype(np.float32, copy=False)


def stage_weights(inp):
    w = {}
    w["l0_w_in_t"] = _t(inp["L0_attn_W_in"])
    w["l0_w_c_t"] = _t(inp["L0_attn_W_c"])
    w["l0_w_d_t"] = _t(inp["L0_attn_W_d"])
    w["l0_w_out_t"] = _t(inp["L0_attn_W_out"])
    w["l0_b_in_t"] = _chunkcol(inp["L0_attn_b_in"], 32)
    cw = np.asarray(inp["L0_attn_conv_w"]).reshape(ECH, 128, 4).transpose(1, 0, 2).reshape(128, 4 * ECH)
    w["l0_convw"] = np.ascontiguousarray(cw).astype(np.float32, copy=False)
    w["l0_convb"] = _chunkcol(inp["L0_attn_conv_b"], ECH)
    w["l0_b_c_t"] = _chunkcol(inp["L0_attn_b_c"], ECH)
    w["l0_b_d_t"] = _chunkcol(inp["L0_attn_b_d"], ECH)
    w["l0_b_out_t"] = _chunkcol(inp["L0_attn_b_out"], 8)
    w["l0_logdelta_t"] = _chunkcol(inp["L0_attn_log_delta"], ECH)
    at = np.asarray(inp["L0_attn_A"]).reshape(ECH, 128, 16).transpose(1, 0, 2).reshape(128, 16 * ECH)
    bt = np.asarray(inp["L0_attn_Bm"]).reshape(ECH, 128, 16).transpose(1, 0, 2).reshape(128, 16 * ECH)
    w["l0_a_t"] = np.ascontiguousarray(at).astype(np.float32, copy=False)
    w["l0_bm_t"] = np.ascontiguousarray(bt).astype(np.float32, copy=False)
    w["l0_ln_g"] = np.asarray(inp["L0_attn_ln_g"]).reshape(1, D).astype(np.float32, copy=False)
    w["l0_ln_b"] = np.asarray(inp["L0_attn_ln_b"]).reshape(1, D).astype(np.float32, copy=False)
    w["l0_wg"] = np.ascontiguousarray(np.asarray(inp["L0_ffn_Wg"])).astype(np.float32, copy=False)
    w["l0_bg4"] = np.tile(np.asarray(inp["L0_ffn_bg"]).reshape(1, 8), (1, 4)).astype(np.float32)
    w["l0_w1"] = np.ascontiguousarray(np.asarray(inp["L0_ffn_W1"])).astype(np.float32, copy=False)
    w["l0_w2"] = np.ascontiguousarray(np.asarray(inp["L0_ffn_W2"])).astype(np.float32, copy=False)
    w["l0_b1_t"] = np.ascontiguousarray(
        np.asarray(inp["L0_ffn_b1"]).reshape(8, 32, 128).transpose(2, 0, 1).reshape(128, 256)).astype(np.float32)
    w["l0_b2_t"] = np.ascontiguousarray(
        np.asarray(inp["L0_ffn_b2"]).reshape(8, 8, 128).transpose(2, 0, 1).reshape(128, 64)).astype(np.float32)
    w["l1_inproj_t"] = _t(inp["L1_attn_in_proj_w"])
    w["l1_dtproj_t"] = _t(inp["L1_attn_dt_proj_w"])
    w["l1_outproj_t"] = _t(inp["L1_attn_out_proj_w"])
    cw1 = np.asarray(inp["L1_attn_conv_w"]).reshape(ECH, 128, 4).transpose(1, 0, 2).reshape(128, 4 * ECH)
    w["l1_convw"] = np.ascontiguousarray(cw1).astype(np.float32, copy=False)
    w["l1_convb"] = _chunkcol(inp["L1_attn_conv_b"], ECH)
    w["l1_dtb_t"] = _chunkcol(inp["L1_attn_dt_proj_b"], ECH)
    w["l1_wr"] = np.ascontiguousarray(np.asarray(inp["L1_ffn_Wr"])).astype(np.float32, copy=False)
    w["l1_w1"] = np.ascontiguousarray(np.asarray(inp["L1_ffn_W1"])).astype(np.float32, copy=False)
    w["l1_w2"] = np.ascontiguousarray(np.asarray(inp["L1_ffn_W2"])).astype(np.float32, copy=False)
    return w


PH1_WEIGHTS = ["l0_w_in_t", "l0_w_c_t", "l0_w_d_t", "l0_w_out_t", "l0_b_in_t",
               "l0_convw", "l0_convb", "l0_b_c_t", "l0_b_d_t", "l0_b_out_t",
               "l0_logdelta_t", "l0_a_t", "l0_bm_t", "l0_ln_g", "l0_ln_b"]
CONST_NAMES = ["ident_bf", "ident_f", "tri_bf", "ones1_bf", "onesc_bf", "ones1_f", "eids",
               "slot_iota", "prefmask", "eids4", "p_iota"]
WSHAPES = {
    "l0_w_in_t": ([D, 4096], F32), "l0_w_c_t": ([E, E], F32), "l0_w_d_t": ([E, E], F32),
    "l0_w_out_t": ([E, D], F32), "l0_b_in_t": ([128, 32], F32), "l0_convw": ([128, 4 * ECH], F32),
    "l0_convb": ([128, ECH], F32), "l0_b_c_t": ([128, ECH], F32), "l0_b_d_t": ([128, ECH], F32),
    "l0_b_out_t": ([128, 8], F32), "l0_logdelta_t": ([128, ECH], F32),
    "l0_a_t": ([128, 16 * ECH], F32), "l0_bm_t": ([128, 16 * ECH], F32),
    "l0_ln_g": ([1, D], F32), "l0_ln_b": ([1, D], F32),
}
CSHAPES = {
    "ident_bf": ([128, 128], BF16), "ident_f": ([128, 128], F32), "tri_bf": ([128, 128], BF16),
    "ones1_bf": ([1, 128], BF16), "onesc_bf": ([128, 1], BF16), "ones1_f": ([1, 128], F32), "eids": ([128, 8], F32),
    "slot_iota": ([128, QC], F32), "prefmask": ([8, 1], F32),
    "eids4": ([1, 32], F32), "p_iota": ([128, 1], F32),
}
L0FFN_WEIGHTS = ["l0_wg", "l0_bg4", "l0_w1", "l0_w2", "l0_b1_t", "l0_b2_t"]
L1FFN_WEIGHTS = ["l1_wr", "l1_w1", "l1_w2"]
L1ATTN_WEIGHTS = ["l1_inproj_t", "l1_dtproj_t", "l1_outproj_t", "l1_convw", "l1_convb", "l1_dtb_t"]
WSHAPES.update({
    "l0_wg": ([D, 8], F32), "l0_bg4": ([1, 32], F32),
    "l0_w1": ([8, D, H], F32), "l0_w2": ([8, H, D], F32),
    "l0_b1_t": ([128, 256], F32), "l0_b2_t": ([128, 64], F32),
    "l1_wr": ([D, 8], F32), "l1_w1": ([8, D, H], F32), "l1_w2": ([8, H, D], F32),
    "l1_inproj_t": ([D, 3072], F32), "l1_dtproj_t": ([E, E], F32), "l1_outproj_t": ([E, D], F32),
    "l1_convw": ([128, 4 * ECH], F32), "l1_convb": ([128, ECH], F32), "l1_dtb_t": ([128, ECH], F32),
})


def build_phase1(dbg_names=()):
    nc = bass.Bass("TRN2", target_bir_lowering=False)
    c = Ctx()
    c.dram = {}
    DBG_SHAPES = {"dbg_xln": ([128, 4 * D], BF16), "dbg_xg": ([128, ECH * TT], BF16),
                  "dbg_sig": ([128, ECH * TT], F32), "dbg_s": ([128, ECH * TT], F32),
                  "dbg_xlnT": ([128, DCH * TT], BF16), "dbg_xr": ([128, 4 * D], F32),
                  "dbg_lnt": ([128, 36], F32), "dbg_mus": ([128, 4], F32), "dbg_m2s": ([128, 4], F32)}
    if dbg_names:
        c.dbg = {}
        for n in dbg_names:
            shp, dt = DBG_SHAPES[n]
            c.dbg[n] = nc.dram_tensor(n, shp, dt, kind="ExternalOutput").ap()
    c.dram["x_in"] = nc.dram_tensor("x_in", [TOK, D], F32, kind="ExternalInput").ap()
    c.dram["xB"] = nc.dram_tensor("xB", [TOK, D], F32, kind="ExternalOutput").ap()
    for n in PH1_WEIGHTS:
        shp, dt = WSHAPES[n]
        c.dram[n] = nc.dram_tensor("d_" + n, shp, dt, kind="ExternalInput").ap()
    for n in CONST_NAMES:
        shp, dt = CSHAPES[n]
        c.dram[n] = nc.dram_tensor("d_" + n, shp, dt, kind="ExternalInput").ap()
    with ExitStack() as stack:
        alloc_common(nc, stack, c)
        emit_l0_attn(c)
        with nc.Block() as block:
            c.p.run(block)
    return nc


def core_slice(x_full, core):
    return np.ascontiguousarray(np.asarray(x_full)[2 * core:2 * core + 2].reshape(TOK, D)).astype(np.float32, copy=False)


def run_phase1(inputs, cores=None):
    cores = cores if cores is not None else list(range(NCORE))
    w = stage_weights(inputs)
    con = const_inputs()
    nc = build_phase1()
    in_maps = []
    for ci, core in enumerate(cores):
        m = {"x_in": core_slice(inputs["L0_attn_x"], core)}
        for k2, v2 in {**w, **con}.items():
            m["d_" + k2] = v2
        m["d_prefmask"] = (np.arange(8) < core).astype(np.float32).reshape(8, 1)
        in_maps.append(m)
    res = run_bass_kernel_spmd(nc, in_maps, core_ids=list(range(len(cores))))
    return [r["xB"] for r in res.results]


def build_phase_ffn(layer, dbg=False):
    nc = bass.Bass("TRN2", target_bir_lowering=False)
    kd = "ExternalOutput" if dbg else "Internal"
    c = Ctx()
    c.dram = {}
    if layer == 0:
        c.dram["xB"] = nc.dram_tensor("xB", [TOK, D], F32, kind="ExternalInput").ap()
        c.dram["xC"] = nc.dram_tensor("xC", [TOK, D], F32, kind="ExternalOutput").ap()
        c.dram["xlnB"] = nc.dram_tensor("xlnB", [TOK, D], BF16, kind=kd).ap()
        c.dram["tabB"] = nc.dram_tensor("tabB", [NEXP * CAP, 2], F32, kind=kd).ap()
        wnames = L0FFN_WEIGHTS
    else:
        c.dram["xD"] = nc.dram_tensor("xD", [TOK, D], F32, kind="ExternalInput").ap()
        c.dram["x_out"] = nc.dram_tensor("x_out", [TOK, D], F32, kind="ExternalOutput").ap()
        c.dram["xlnD"] = nc.dram_tensor("xlnD", [TOK, D], BF16, kind=kd).ap()
        c.dram["tabD"] = nc.dram_tensor("tabD", [NEXP * CAP, 2], F32, kind=kd).ap()
        c.dram["cc_in"] = nc.dram_tensor("cc_in", [1, 8], F32).ap()
        c.dram["cc_out"] = nc.dram_tensor("cc_out", [8, 8], F32).ap()
        c.dram["thr_d"] = nc.dram_tensor("thr_d", [1, 8], F32).ap()
        wnames = L1FFN_WEIGHTS
    for n in wnames:
        shp, dt = WSHAPES[n]
        c.dram[n] = nc.dram_tensor("d_" + n, shp, dt, kind="ExternalInput").ap()
    for n in CONST_NAMES:
        shp, dt = CSHAPES[n]
        c.dram[n] = nc.dram_tensor("d_" + n, shp, dt, kind="ExternalInput").ap()
    if dbg == "e0":
        c.only_expert = 1
        c.only_passes = [(0, 2)]
    if dbg:
        c.edbg = {}
        for nm, shp, dt in [("edbg_xlnT", [128, DCH * 512], BF16), ("edbg_h", [128, HCH * 512], BF16),
                            ("edbg_out", [128, 4 * D], F32), ("edbg_xg", [128, 4 * D], F32),
                            ("edbg_pe", [128, QC], F32), ("edbg_tab", [128, QC * 2], F32),
                            ("edbg_idxi", [128, QC], I32), ("edbg_xlg", [128, 4 * D], BF16),
                            ("edbg_o", [128, 4 * 512], F32), ("edbg_wb", [128, 4096], BF16)]:
            c.edbg[nm] = nc.dram_tensor(nm, shp, dt, kind="ExternalOutput").ap()
    with ExitStack() as stack:
        alloc_common(nc, stack, c)
        gout = emit_gate(c, layer)
        emit_expert(c, layer, thr_bc=gout.get("thr_bc"))
        with nc.Block() as block:
            c.p.run(block)
    return nc


def run_phase_ffn(layer, x_full, weights, cores=None):
    cores = cores if cores is not None else list(range(NCORE))
    con = const_inputs()
    nc = build_phase_ffn(layer)
    in_maps = []
    key_in = "xB" if layer == 0 else "xD"
    key_out = "xC" if layer == 0 else "x_out"
    for core in cores:
        m = {key_in: core_slice(x_full, core)}
        for k2 in (L0FFN_WEIGHTS if layer == 0 else L1FFN_WEIGHTS):
            m["d_" + k2] = weights[k2]
        for k2, v2 in con.items():
            m["d_" + k2] = v2
        m["d_prefmask"] = (np.arange(8) < core).astype(np.float32).reshape(8, 1)
        in_maps.append(m)
    res = run_bass_kernel_spmd(nc, in_maps, core_ids=list(range(len(cores))))
    return [r[key_out] for r in res.results]


def build_phase_l1attn():
    nc = bass.Bass("TRN2", target_bir_lowering=False)
    c = Ctx()
    c.dram = {}
    c.dram["xC"] = nc.dram_tensor("xC", [TOK, D], F32, kind="ExternalInput").ap()
    c.dram["xD"] = nc.dram_tensor("xD", [TOK, D], F32, kind="ExternalOutput").ap()
    for n in L1ATTN_WEIGHTS:
        shp, dt = WSHAPES[n]
        c.dram[n] = nc.dram_tensor("d_" + n, shp, dt, kind="ExternalInput").ap()
    for n in CONST_NAMES:
        shp, dt = CSHAPES[n]
        c.dram[n] = nc.dram_tensor("d_" + n, shp, dt, kind="ExternalInput").ap()
    with ExitStack() as stack:
        alloc_common(nc, stack, c)
        emit_l1_attn(c)
        with nc.Block() as block:
            c.p.run(block)
    return nc


def run_phase_l1attn(x_full, weights, cores=None):
    cores = cores if cores is not None else list(range(NCORE))
    con = const_inputs()
    nc = build_phase_l1attn()
    in_maps = []
    for core in cores:
        m = {"xC": core_slice(x_full, core)}
        for k2 in L1ATTN_WEIGHTS:
            m["d_" + k2] = weights[k2]
        for k2, v2 in con.items():
            m["d_" + k2] = v2
        m["d_prefmask"] = (np.arange(8) < core).astype(np.float32).reshape(8, 1)
        in_maps.append(m)
    res = run_bass_kernel_spmd(nc, in_maps, core_ids=list(range(len(cores))))
    return [r["xD"] for r in res.results]


_CACHE = {}


def kernel(**inputs):
    w = stage_weights(inputs)
    con = const_inputs()
    if "p1" not in _CACHE:
        _CACHE["p1"] = build_phase1()
        _CACHE["f0"] = build_phase_ffn(0)
        _CACHE["l1"] = build_phase_l1attn()
        _CACHE["f1"] = build_phase_ffn(1)
    x_full = np.asarray(inputs["L0_attn_x"], np.float32)

    def phase_maps(key_in, x_slices, wnames):
        maps = []
        for core in range(NCORE):
            m = {key_in: x_slices[core]}
            for k2 in wnames:
                m["d_" + k2] = w[k2]
            for k2, v2 in con.items():
                m["d_" + k2] = v2
            m["d_prefmask"] = (np.arange(8) < core).astype(np.float32).reshape(8, 1)
            maps.append(m)
        return maps

    xs = [core_slice(x_full, cc) for cc in range(NCORE)]
    r1 = run_bass_kernel_spmd(_CACHE["p1"], phase_maps("x_in", xs, PH1_WEIGHTS),
                              core_ids=list(range(NCORE)))
    xb = [r["xB"] for r in r1.results]
    r2 = run_bass_kernel_spmd(_CACHE["f0"], phase_maps("xB", xb, L0FFN_WEIGHTS),
                              core_ids=list(range(NCORE)))
    xc = [r["xC"] for r in r2.results]
    r3 = run_bass_kernel_spmd(_CACHE["l1"], phase_maps("xC", xc, L1ATTN_WEIGHTS),
                              core_ids=list(range(NCORE)))
    xd = [r["xD"] for r in r3.results]
    r4 = run_bass_kernel_spmd(_CACHE["f1"], phase_maps("xD", xd, L1FFN_WEIGHTS),
                              core_ids=list(range(NCORE)))
    out = np.concatenate([r["x_out"] for r in r4.results], axis=0)
    return out.reshape(16, 4096, 1024)
